# version 21
# baseline (speedup 1.0000x reference)
import numpy as np
import ml_dtypes
from collections import defaultdict

import concourse.bass as bass
import concourse.mybir as mybir
from concourse.bass_utils import run_bass_kernel_spmd

F32 = mybir.dt.float32
BF16 = mybir.dt.bfloat16
I32 = mybir.dt.int32
I16 = mybir.dt.int16
AF = mybir.ActivationFunctionType
ALU = mybir.AluOpType
AX = mybir.AxisListType

BB, SS, EE = 2, 8192, 2048
H, HKV, DH = 16, 4, 128
NMLP = 16384
BLK = 512
NB = SS // BLK
ECH = EE // 128
SCALE = DH ** -0.5
GSH = 2048
DSH = 256
SH_G = 2048 * 2048
SH_D = 16384 * 256
SH = SH_G * 2 + SH_D

ENGINES = ("sync", "scalar", "vector", "tensor", "gpsimd")


def _key_of(x):
    if isinstance(x, str):
        return x
    t = getattr(x, "tensor", None)
    if t is not None:
        return t.name
    return x.name


class _Chan:
    __slots__ = ("name", "queue", "sem", "count", "mode", "key")

    def __init__(self, nc, name, queue, mode):
        self.name = name
        self.queue = queue
        self.sem = nc.alloc_semaphore(name=f"ch_{name}")
        self.count = 0
        self.mode = mode
        self.key = None


class _Tok:
    __slots__ = ("eng", "sem", "val", "chan")

    def __init__(self, eng, sem=None, val=0, chan=None):
        self.eng = eng
        self.sem = sem
        self.val = val
        self.chan = chan

    def resolve(self):
        if self.chan is not None:
            if self.chan.mode == "bulk":
                return self.chan.sem, 16 * self.chan.count
            return self.chan.sem, 16 * self.val
        return self.sem, self.val


class _Eng:
    def __init__(self, nc, name):
        self.name = name
        self.sem = nc.alloc_semaphore(name=f"sem_{name}")
        self.count = 0
        self.waited = defaultdict(int)
        self.items = []


class Prog:
    def __init__(self, nc):
        self.nc = nc
        self.eng = {n: _Eng(nc, n) for n in ENGINES}
        self.chans = {}
        self.last_w = {}
        self.readers = defaultdict(list)
        self.finalized = False

    def chan(self, name, queue, mode="ring"):
        c = _Chan(self.nc, name, queue, mode)
        self.chans[name] = c
        return c

    def _wait(self, e, tok, waits):
        sem, val = tok.resolve()
        if e.waited[id(sem)] < val:
            waits.append((sem, val))
            e.waited[id(sem)] = val

    def _deps(self, engname, e, rkeys, wkeys, pe_relaxed):
        waits = []
        for k in rkeys:
            tok = self.last_w.get(k)
            if tok is not None and not (pe_relaxed and tok.eng == engname == "tensor"):
                self._wait(e, tok, waits)
        for k in wkeys:
            tok = self.last_w.get(k)
            if tok is not None and not (pe_relaxed and tok.eng == engname == "tensor"):
                self._wait(e, tok, waits)
            for rtok in self.readers.get(k, ()):
                if pe_relaxed and rtok.eng == engname == "tensor":
                    continue
                self._wait(e, rtok, waits)
        return waits

    def _record(self, tok, rkeys, wkeys):
        for k in wkeys:
            self.last_w[k] = tok
            self.readers[k] = []
        for k in rkeys:
            self.readers[k].append(tok)

    def op(self, engname, fn, reads=None, writes=None, pe_relaxed=True):
        return self.op_group(engname, [fn], reads, writes, pe_relaxed)

    def op_group(self, engname, fns, reads=None, writes=None, pe_relaxed=True):
        assert not self.finalized
        e = self.eng[engname]
        rkeys = [_key_of(r) for r in (reads or [])]
        wkeys = [_key_of(w) for w in (writes or [])]
        waits = self._deps(engname, e, rkeys, wkeys, pe_relaxed)
        e.count += 1
        tok = _Tok(engname, sem=e.sem, val=e.count)
        self._record(tok, rkeys, wkeys)
        n = len(fns)
        for i, fn in enumerate(fns):
            if i == n - 1:
                e.items.append((fn, waits if i == 0 else (), e.sem, 1))
            else:
                e.items.append((fn, waits if i == 0 else (), None, 0))
        return tok

    def dma(self, ch, out, in_, reads=None, writes=None, **kw):
        assert not self.finalized
        if isinstance(ch, str):
            ch = self.chans[ch]
        e = self.eng[ch.queue]
        rkeys = [_key_of(r) for r in (reads if reads is not None else [in_])]
        wkeys = [_key_of(w) for w in (writes if writes is not None else [out])]
        if ch.mode == "ring":
            k = tuple(sorted(wkeys))
            if ch.key is None:
                ch.key = k
            else:
                assert ch.key == k, f"ring chan {ch.name}: {k} != {ch.key}"
        waits = self._deps(ch.queue, e, rkeys, wkeys, pe_relaxed=False)
        ch.count += 1
        tok = _Tok(ch.queue, chan=ch, val=ch.count)
        self._record(tok, rkeys, wkeys)

        def fn(eng, _o=out, _i=in_, _kw=kw):
            return eng.dma_start(out=_o, in_=_i, **_kw)

        e.items.append((fn, waits, ch.sem, 16))
        return tok

    def wait_tokens(self, engname, tokens):
        e = self.eng[engname]
        waits = []
        for tok in tokens:
            self._wait(e, tok, waits)
        if waits:
            e.items.append((None, waits, None, 0))

    def wait_keys(self, engname, keys):
        toks = [self.last_w[_key_of(k)] for k in keys if _key_of(k) in self.last_w]
        self.wait_tokens(engname, toks)

    def barrier_all(self, limit=None):
        toks = []
        for n in ENGINES:
            e = self.eng[n]
            cnt = min(e.count, limit.get(n, e.count)) if limit else e.count
            if cnt:
                toks.append(_Tok(n, sem=e.sem, val=cnt))
        for c in self.chans.values():
            if c.count:
                toks.append(_Tok(c.queue, chan=c, val=c.count))
        for n in ENGINES:
            self.wait_tokens(n, toks)

    def barrier_token(self, ch):
        if isinstance(ch, str):
            ch = self.chans[ch]
        return _Tok(ch.queue, chan=ch, val=ch.count)

    def finalize(self):
        assert not self.finalized
        self.finalized = True
        block = self.nc.Block()
        block.__enter__()
        for name in ENGINES:
            e = self.eng[name]
            if not e.items:
                continue
            section = getattr(block, name)

            def emit(eng, _e=e):
                for fn, waits, sem, amt in _e.items:
                    for wsem, wval in waits:
                        eng.wait_ge(wsem, wval)
                    if fn is not None:
                        inst = fn(eng)
                        if sem is not None:
                            inst.then_inc(sem, amt)

            section(emit)
        block.__exit__(None, None, None)



def build(debug=0, phases=3):
    nc = bass.Bass("TRN2", target_bir_lowering=False)

    x_full = nc.dram_tensor("x_full", [SS, EE], F32, kind="ExternalInput")
    x_own = nc.dram_tensor("x_own", [2048, EE], F32, kind="ExternalInput")
    wq_g = nc.dram_tensor("wq_g", [512, EE], F32, kind="ExternalInput")
    bq_g = nc.dram_tensor("bq_g", [1, 512], F32, kind="ExternalInput")
    wk_g = nc.dram_tensor("wk_g", [128, EE], F32, kind="ExternalInput")
    bk_g = nc.dram_tensor("bk_g", [1, 128], F32, kind="ExternalInput")
    wv_g = nc.dram_tensor("wv_g", [128, EE], F32, kind="ExternalInput")
    bv_g = nc.dram_tensor("bv_g", [1, 128], F32, kind="ExternalInput")
    wo_t = nc.dram_tensor("wo", [EE, EE], F32, kind="ExternalInput")
    bo_t = nc.dram_tensor("bo", [1, EE], F32, kind="ExternalInput")
    gate_pk = nc.dram_tensor("gate_pk", [GSH, 1024], I32, kind="ExternalInput")
    gate_sc = nc.dram_tensor("gate_sc", [GSH, 64], F32, kind="ExternalInput")
    up_pk = nc.dram_tensor("up_pk", [GSH, 1024], I32, kind="ExternalInput")
    up_sc = nc.dram_tensor("up_sc", [GSH, 64], F32, kind="ExternalInput")
    down_pk = nc.dram_tensor("down_pk", [DSH, 8192], I32, kind="ExternalInput")
    down_sc = nc.dram_tensor("down_sc", [DSH, 512], F32, kind="ExternalInput")
    identb_t = nc.dram_tensor("identb", [128, 128], BF16, kind="ExternalInput")
    identf_t = nc.dram_tensor("identf", [128, 128], F32, kind="ExternalInput")
    ropec_t = nc.dram_tensor("ropec", [64, SS], F32, kind="ExternalInput")
    ropes_t = nc.dram_tensor("ropes", [64, SS], F32, kind="ExternalInput")
    maskm_t = nc.dram_tensor("maskm", [128, 896], BF16, kind="ExternalInput")
    sel0_t = nc.dram_tensor("sel0", [128, 1], F32, kind="ExternalInput")
    sel1_t = nc.dram_tensor("sel1", [128, 1], F32, kind="ExternalInput")

    out_t = nc.dram_tensor("out", [2048, EE], F32, kind="ExternalOutput")

    ffn_shard = nc.dram_tensor("ffn_shard", [SH], BF16)
    ffn_full = nc.dram_tensor("ffn_full", [8 * SH], BF16)
    woT_scr = nc.dram_tensor("woT_scr", [16 * 16 * 128 * 128], BF16)
    qT_scr = nc.dram_tensor("qT_scr", [512, SS], BF16)
    a2a_in = nc.dram_tensor("a2a_in", [8, 512, 2048], BF16)
    a2a_out = nc.dram_tensor("a2a_out", [8, 512, 2048], BF16)

    dbg = {}
    if debug:
        dbg["d_wqT"] = nc.dram_tensor("d_wqT", [128, 16 * 512], BF16, kind="ExternalOutput")
        dbg["d_ffn"] = nc.dram_tensor("d_ffn", [8 * SH], BF16, kind="ExternalOutput")
        dbg["d_woT"] = nc.dram_tensor("d_woT", [16 * 16 * 128 * 128], BF16, kind="ExternalOutput")
        if phases >= 1:
            dbg["d_kT"] = nc.dram_tensor("d_kT", [128, SS], BF16, kind="ExternalOutput")
            dbg["d_v"] = nc.dram_tensor("d_v", [128, SS], BF16, kind="ExternalOutput")
            dbg["d_qT"] = nc.dram_tensor("d_qT", [512, SS], BF16, kind="ExternalOutput")
            dbg["d_xnT"] = nc.dram_tensor("d_xnT", [128, ECH * 2048], BF16, kind="ExternalOutput")
            dbg["d_B"] = nc.dram_tensor("d_B", [128, 1], F32, kind="ExternalOutput")
        if phases >= 2:
            dbg["d_a2a"] = nc.dram_tensor("d_a2a", [8, 512, 2048], BF16, kind="ExternalOutput")

    P = Prog(nc)
    groups = defaultdict(list)
    alloc_order = []

    def sb(name, shape, dtype, grp):
        h = nc.sbuf_tensor(name, shape, dtype)
        t = h.__enter__()
        groups[grp].append(h)
        alloc_order.append(h)
        return t

    def free_group(grp):
        for h in reversed(groups.pop(grp, [])):
            h.__exit__(None, None, None)
            alloc_order.remove(h)

    def psum(name, shape, dtype):
        h = nc.psum_tensor(name, shape, dtype)
        t = h.__enter__()
        groups["psum"].append(h)
        return t

    identb = sb("identb_s", [128, 128], BF16, "persist")
    identf = sb("identf_s", [128, 128], F32, "persist")
    maskm = sb("maskm_s", [128, 896], BF16, "persist")
    onescol_b = sb("onescol_b", [128, 1], BF16, "persist")
    onesrow_b = sb("onesrow_b", [1, BLK], BF16, "persist")
    onesrow_f = sb("onesrow_f", [1, 128], F32, "persist")

    bq_sb = sb("bq_sb", [1, 512], BF16, "persist")
    bk_sb = sb("bk_sb", [1, 128], BF16, "persist")
    bv_sb = sb("bv_sb", [1, 128], BF16, "persist")
    bo_sb = sb("bo_sb", [1, EE], BF16, "persist")
    sel0 = sb("sel0_s", [128, 1], F32, "persist")
    sel1 = sb("sel1_s", [128, 1], F32, "persist")
    negB = sb("negB", [128, 1], F32, "persist")
    xnT_own = sb("xnT_own", [128, ECH * 2048], BF16, "persist")

    K_sb = sb("K_sb", [128, SS], BF16, "p12")
    V_sb = sb("V_sb", [128, SS], BF16, "p12")

    PS = [psum(f"ps{i}", [128, 1024], BF16) for i in range(8)]
    PSF = [p[:].bitcast(F32) for p in PS]
    psA, psB = PS[6], PS[7]
    psA_f, psB_f = PSF[6], PSF[7]
    psM, psM_k = PSF[0], PS[0]
    psU, psU_k = PSF[1], PS[1]
    psO, psO_k = PSF[4], PS[4]
    psS, psS_k = PSF[5][0:1, :], PS[5]
    psR, psR_k = PSF[3], PS[3]

    P.chan("c_const", "sync", "bulk")
    P.chan("c_x", "sync", "bulk")
    P.chan("c_w", "sync", "bulk")
    P.chan("c_rope", "sync", "bulk")
    P.chan("c_spill", "gpsimd", "bulk")
    P.chan("c_q", "gpsimd", "bulk")
    P.chan("c_oT", "gpsimd", "bulk")
    P.chan("c_fw", "sync", "bulk")
    P.chan("c_fd", "sync", "bulk")
    P.chan("c_xo", "gpsimd", "bulk")
    P.chan("c_out", "gpsimd", "bulk")

    P.dma("c_const", out=identb[:], in_=identb_t[:])
    P.dma("c_const", out=identf[:], in_=identf_t[:])
    P.dma("c_const", out=maskm[:], in_=maskm_t[:])
    P.dma("c_const", out=sel0[:], in_=sel0_t[:])
    P.dma("c_const", out=sel1[:], in_=sel1_t[:])
    P.op("vector", lambda e: e.memset(onescol_b[:], 1.0), [], [onescol_b])
    P.op("vector", lambda e: e.memset(onesrow_b[:], 1.0), [], [onesrow_b])
    P.op("vector", lambda e: e.memset(onesrow_f[:], 1.0), [], [onesrow_f])
    for dst, srct in [(bq_sb, bq_g), (bk_sb, bk_g), (bv_sb, bv_g), (bo_sb, bo_t)]:
        P.dma("c_spill", out=dst[:], in_=srct[:])

    wqT = sb("wqT", [128, ECH * 512], BF16, "p1x")
    wkT = sb("wkT", [128, ECH * 128], BF16, "p1x")
    wvT = sb("wvT", [128, ECH * 128], BF16, "p1x")
    wbuf = [sb(f"wbuf{i}", [128, EE], F32, "p0") for i in range(2)]
    wbufb = sb("wbufb", [128, EE], BF16, "p0")
    pkb = [sb(f"pkb{i}", [128, 1024], I32, "p0") for i in range(2)]
    scb = [sb(f"scb{i}", [128, 64], F32, "p0") for i in range(2)]
    scbb = sb("scbb", [128, 64], BF16, "p0")
    dqw = [sb(f"dqw{i}", [128, 1024], I16, "p0") for i in range(2)]
    dqlo = [sb(f"dqlo{i}", [128, 1024], I16, "p0") for i in range(2)]
    dqhi = [sb(f"dqhi{i}", [128, 1024], I16, "p0") for i in range(2)]
    dqnib = [sb(f"dqnib{i}", [128, 2048], I16, "p0") for i in range(2)]
    dqm = sb("dqm", [128, 2048], I16, "p0")
    dqsgn = sb("dqsgn", [128, 2048], I16, "p0")
    dqge = sb("dqge", [128, 2048], I16, "p0")
    dqbits = sb("dqbits", [128, 2048], I16, "p0")
    dqval = sb("dqval", [128, 2048], BF16, "p0")
    asm = [sb(f"asm{i}", [128, 2048], BF16, "p0") for i in range(2)]

    def prep_wT(src, rows, dst):
        for t in range(rows // 128):
            wb = wbuf[t % 2]
            P.dma("c_w", out=wb[:], in_=src[t * 128:(t + 1) * 128, :], writes=[wb])
            P.op("vector", lambda e, wb=wb: e.tensor_copy(wbufb[:], wb[:]), [wb], [wbufb])
            for ec in range(ECH):
                ps = psA if ec % 2 == 0 else psB
                P.op("tensor", lambda e, ec=ec, ps=ps: e.transpose(ps[:, 0:128], wbufb[:, ec * 128:(ec + 1) * 128], identb[:]),
                     [wbufb, identb], [ps])
                c0 = ec * rows + t * 128
                P.op("vector", lambda e, ps=ps, c0=c0, dst=dst: e.tensor_copy(dst[:, c0:c0 + 128], ps[:, 0:128]), [ps], [dst])

    prep_wT(wq_g, 512, wqT)
    prep_wT(wk_g, 128, wkT)
    prep_wT(wv_g, 128, wvT)

    woT_v = woT_scr[:].rearrange("(oc p c) -> oc p c", oc=16, p=128)
    for t in range(16):
        wb = wbuf[t % 2]
        am = asm[t % 2]
        P.dma("c_w", out=wb[:], in_=wo_t[t * 128:(t + 1) * 128, :], writes=[wb])
        P.op("vector", lambda e, wb=wb: e.tensor_copy(wbufb[:], wb[:]), [wb], [wbufb])
        for ic in range(16):
            ps = psA if ic % 2 == 0 else psB
            P.op("tensor", lambda e, ic=ic, ps=ps: e.transpose(ps[:, 0:128], wbufb[:, ic * 128:(ic + 1) * 128], identb[:]),
                 [wbufb, identb], [ps])
            P.op("vector", lambda e, ic=ic, ps=ps, am=am: e.tensor_copy(am[:, ic * 128:(ic + 1) * 128], ps[:, 0:128]), [ps], [am])
        P.dma("c_spill", out=woT_v[t], in_=am[:], reads=[am], writes=[f"woT#{t}"])

    shard_keys = []

    def dequant_tile(pk_src, sc_src, rt, piece_dst):
        pb = pkb[rt % 2]
        scf = scb[rt % 2]
        P.dma("c_w", out=pb[:], in_=pk_src, writes=[pb])
        P.dma("c_w", out=scf[:], in_=sc_src, writes=[scf])
        dw, dl, dh, dn = dqw[rt % 2], dqlo[rt % 2], dqhi[rt % 2], dqnib[rt % 2]
        P.op("gpsimd", lambda e, pb=pb, dw=dw: e.tensor_copy(dw[:], pb[:]), [pb], [dw])
        P.op("vector", lambda e, dw=dw, dl=dl: e.tensor_scalar(dl[:], dw[:], 15, None, ALU.bitwise_and), [dw], [dl])
        P.op("vector", lambda e, dw=dw, dh=dh: e.tensor_scalar(dh[:], dw[:], 4, 15, ALU.logical_shift_right, ALU.bitwise_and), [dw], [dh])
        nib_v = dn[:].rearrange("p (i two) -> p i two", two=2)
        P.op("gpsimd", lambda e, nib_v=nib_v, dl=dl: e.tensor_copy(nib_v[:, :, 0], dl[:]), [dl], [dn])
        P.op("gpsimd", lambda e, nib_v=nib_v, dh=dh: e.tensor_copy(nib_v[:, :, 1], dh[:]), [dh], [dn])
        P.op("vector", lambda e, dn=dn: e.tensor_scalar(dqm[:], dn[:], 7, None, ALU.bitwise_and), [dn], [dqm])
        P.op("vector", lambda e, dn=dn: e.tensor_scalar(dqsgn[:], dn[:], 8, 12, ALU.bitwise_and, ALU.logical_shift_left), [dn], [dqsgn])
        P.op("vector", lambda e: e.tensor_scalar(dqbits[:], dqm[:], 16128, None, ALU.mult), [dqm], [dqbits])
        P.op("vector", lambda e, dn=dn: e.tensor_scalar(dn[:], dqm[:], 64, 16128, ALU.mult, ALU.add), [dqm], [dn])
        P.op("vector", lambda e: e.tensor_scalar(dqge[:], dqm[:], 2, None, ALU.is_ge), [dqm], [dqge])
        P.op("vector", lambda e, dn=dn: e.copy_predicated(dqbits[:], dqge[:], dn[:]), [dqge, dn, dqbits], [dqbits])
        P.op("vector", lambda e: e.tensor_tensor(dqbits[:], dqbits[:], dqsgn[:], ALU.bitwise_or), [dqbits, dqsgn], [dqbits])
        P.op("vector", lambda e, scf=scf: e.tensor_copy(scbb[:], scf[:]), [scf], [scbb])
        sclb = scbb[:].unsqueeze(-1).broadcast_to([128, 64, 32])
        vv = dqbits[:].bitcast(BF16).rearrange("p (i b) -> p i b", b=32)
        ov = dqval[:].rearrange("p (i b) -> p i b", b=32)
        P.op("vector", lambda e, vv=vv, ov=ov, sclb=sclb: e.tensor_tensor(ov, vv, sclb, ALU.mult), [dqbits, scbb], [dqval])
        am = asm[rt % 2]
        for sub in range(16):
            ps = psA if sub % 2 == 0 else psB
            P.op("tensor", lambda e, sub=sub, ps=ps: e.transpose(ps[:, 0:128], dqval[:, sub * 128:(sub + 1) * 128], identb[:]),
                 [dqval, identb], [ps])
            P.op("vector", lambda e, sub=sub, ps=ps, am=am: e.tensor_copy(am[:, sub * 128:(sub + 1) * 128], ps[:, 0:128]), [ps], [am])
        dst, key = piece_dst(-1)
        shard_keys.append(key)
        src_ap = am[:] if dst.ndim == 2 else am[:].rearrange("p (ncl h) -> p ncl h", ncl=16)
        P.dma("c_spill", out=dst, in_=src_ap, reads=[am], writes=[key])

    for gu, pk_t, sc_t in [(0, gate_pk, gate_sc), (1, up_pk, up_sc)]:
        for nt in range(16):
            def gdst(sub, nt=nt, gu=gu):
                off = nt * 524288 + gu * 262144
                blk = ffn_shard[off: off + 262144]
                return (blk.rearrange("(p c) -> p c", p=128), f"ffs#{gu}#{nt}")

            dequant_tile(pk_t[nt * 128:(nt + 1) * 128, :], sc_t[nt * 128:(nt + 1) * 128, :], nt + 16 * gu, gdst)

    for ht in range(2):
        for cp in range(8):
            def ddst(sub, ht=ht, cp=cp):
                blk = ffn_shard[2 * SH_G + cp * 524288: 2 * SH_G + (cp + 1) * 524288]
                v = blk.rearrange("(p ncl h) -> p ncl h", p=128, ncl=16)
                return (v[:, :, ht * 128:(ht + 1) * 128], f"ffs#d#{cp}#{ht}")

            dequant_tile(down_pk[ht * 128:(ht + 1) * 128, cp * 1024:(cp + 1) * 1024],
                         down_sc[ht * 128:(ht + 1) * 128, cp * 64:(cp + 1) * 64], ht * 8 + cp, ddst)

    P.wait_keys("gpsimd", shard_keys)
    _snap0 = P.eng["gpsimd"].count
    P.op("gpsimd", lambda e: e.collective_compute(
        "AllGather", ALU.bypass, replica_groups=[list(range(8))],
        ins=[ffn_shard[:].opt()], outs=[ffn_full[:].opt()]),
        shard_keys, [ffn_full])

    if debug:
        P.dma("c_spill", out=dbg["d_wqT"][:], in_=wqT[:], reads=[wqT])
        P.dma("c_spill", out=dbg["d_ffn"][:], in_=ffn_full[:], reads=[ffn_full])
        P.dma("c_spill", out=dbg["d_woT"][:], in_=woT_scr[:], reads=[f"woT#{t}" for t in range(16)])

    P.barrier_all(limit={"gpsimd": _snap0})
    free_group("p0")

    if phases >= 1:
        xtile = [sb(f"xtile{i}", [128, EE], F32, "p1") for i in range(2)]
        lnscr = sb("lnscr", [128, EE], F32, "p1")
        ln_s = [sb(f"ln_s{i}", [128, 1], F32, "p1") for i in range(2)]
        ln_ss = [sb(f"ln_ss{i}", [128, 1], F32, "p1") for i in range(2)]
        ln_mu = [sb(f"ln_mu{i}", [128, 1], F32, "p1") for i in range(2)]
        ln_m2 = [sb(f"ln_m2{i}", [128, 1], F32, "p1") for i in range(2)]
        ln_r = [sb(f"ln_r{i}", [128, 1], F32, "p1") for i in range(2)]
        xnt_tok = [sb(f"xnt_tok{i}", [128, EE], BF16, "p1") for i in range(2)]
        xnT_cur = sb("xnT_cur", [128, ECH * BLK], BF16, "p1")
        ropec = [sb(f"ropecs{i}", [64, BLK], F32, "p1") for i in range(2)]
        ropes = [sb(f"ropess{i}", [64, BLK], F32, "p1") for i in range(2)]
        rp1 = [sb(f"rp1_{i}", [64, BLK], F32, "p1") for i in range(2)]
        rp2 = [sb(f"rp2_{i}", [64, BLK], F32, "p1") for i in range(2)]
        qspill = [sb(f"qspill{i}", [128, BLK], BF16, "p1") for i in range(4)]
        nsq = sb("nsq", [128, BLK], BF16, "p1")
        nmax = sb("nmax", [1, 1], F32, "p1")
        nmax_q = sb("nmax_q", [1, 1], F32, "p1")
        nmax_k = sb("nmax_k", [1, 1], F32, "p1")
        ntmp = sb("ntmp", [1, 1], F32, "p1")

        def ln_block(x_src_rows, ti):
            par = ti % 2
            xt = xtile[ti % 2]
            sc, s_, ss, mu, m2, r_ = lnscr, ln_s[par], ln_ss[par], ln_mu[par], ln_m2[par], ln_r[par]
            xo = xnt_tok[par]
            P.dma("c_x", out=xt[:], in_=x_src_rows, writes=[xt])
            P.op("vector", lambda e, xt=xt, s_=s_: e.tensor_reduce(s_[:], xt[:], AX.X, ALU.add), [xt], [s_])
            P.op("scalar", lambda e, xt=xt, sc=sc, ss=ss: e.activation(sc[:], xt[:], AF.Square, accum_out=ss[:]), [xt], [sc, ss])
            P.op("vector", lambda e, s_=s_, mu=mu: e.tensor_scalar(mu[:], s_[:], 1.0 / EE, None, ALU.mult), [s_], [mu])
            P.op("vector", lambda e, mu=mu, m2=m2: e.tensor_scalar(m2[:], mu[:], mu[:], 1e-5, ALU.mult, ALU.subtract), [mu], [m2])
            P.op("vector", lambda e, ss=ss, m2=m2: e.scalar_tensor_tensor(m2[:], ss[:], 1.0 / EE, m2[:], ALU.mult, ALU.subtract), [ss, m2], [m2])
            P.op("scalar", lambda e, m2=m2: e.activation(m2[:], m2[:], AF.Sqrt), [m2], [m2])
            P.op("vector", lambda e, m2=m2, r_=r_: e.reciprocal(r_[:], m2[:]), [m2], [r_])
            P.op("vector", lambda e, xt=xt, xo=xo, mu=mu, r_=r_: e.tensor_scalar(xo[:], xt[:], mu[:], r_[:], ALU.subtract, ALU.mult), [xt, mu, r_], [xo])
            return xo

        trring = [psA, psB]

        def transpose_to(xsrc, dst, col_base, tsub):
            for ec in range(ECH):
                ps = trring[ec % 2]
                P.op("tensor", lambda e, ec=ec, ps=ps, xsrc=xsrc: e.transpose(ps[:, 0:128], xsrc[:, ec * 128:(ec + 1) * 128], identb[:]),
                     [xsrc, identb], [ps])
                c0 = col_base(ec) + tsub * 128
                P.op("vector", lambda e, ps=ps, c0=c0, dst=dst: e.tensor_copy(dst[:, c0:c0 + 128], ps[:, 0:128]), [ps], [dst])

        def rope_apply(ps_src, pskey, dst, dst_key, c0, c1, rc, rs, par):
            r1, r2 = rp1[par], rp2[par]
            P.op("vector", lambda e, r1=r1: e.tensor_tensor(r1[:], ps_src[0:64, :], rc[0:64, :], ALU.mult), [pskey, rc], [r1])
            P.op("vector", lambda e, r2=r2: e.tensor_tensor(r2[:], ps_src[64:128, :], rs[0:64, :], ALU.mult), [pskey, rs], [r2])
            P.op("vector", lambda e, r1=r1, r2=r2: e.tensor_tensor(dst[0:64, c0:c1], r1[:], r2[:], ALU.subtract), [r1, r2], [dst_key])
            P.op("vector", lambda e, r1=r1: e.tensor_tensor(r1[:], ps_src[64:128, :], rc[0:64, :], ALU.mult), [pskey, rc], [r1])
            P.op("vector", lambda e, r2=r2: e.tensor_tensor(r2[:], ps_src[0:64, :], rs[0:64, :], ALU.mult), [pskey, rs], [r2])
            P.op("vector", lambda e, r1=r1, r2=r2: e.tensor_tensor(dst[64:128, c0:c1], r1[:], r2[:], ALU.add), [r1, r2], [dst_key])

        def norm_track(src_ap, src_keys, tgt):
            P.op("vector", lambda e: e.tensor_tensor(nsq[:], src_ap, src_ap, ALU.mult), src_keys, [nsq])
            P.op("tensor", lambda e: e.matmul(psS[0:1, 0:BLK], onescol_b[:], nsq[:], start=True, stop=True), [onescol_b, nsq], [psS_k])
            P.op("vector", lambda e: e.tensor_reduce(nmax[:], psS[0:1, 0:BLK], AX.X, ALU.max), [psS_k], [nmax])
            P.op("vector", lambda e, tgt=tgt: e.tensor_tensor(tgt[:], tgt[:], nmax[:], ALU.max), [nmax, tgt], [tgt])

        P.op("vector", lambda e: e.memset(nmax_q[:], 0.0), [], [nmax_q])
        P.op("vector", lambda e: e.memset(nmax_k[:], 0.0), [], [nmax_k])
        psV, psV_k = PSF[2], PS[2]
        psQ2, psQ2_k = PSF[4], PS[4]
        for tb in range(NB):
            rc = ropec[tb % 2]
            rs = ropes[tb % 2]
            P.dma("c_rope", out=rc[:], in_=ropec_t[:, tb * BLK:(tb + 1) * BLK], writes=[rc])
            P.dma("c_rope", out=rs[:], in_=ropes_t[:, tb * BLK:(tb + 1) * BLK], writes=[rs])
            for tsub in range(4):
                r0 = tb * BLK + tsub * 128
                xo = ln_block(x_full[r0:r0 + 128, :], tb * 4 + tsub)
                transpose_to(xo, xnT_cur, lambda ec: ec * BLK, tsub)
                fns = []
                for ec in range(ECH):
                    fns.append(lambda e, ec=ec, tsub=tsub: e.matmul(
                        psV[:, 0:128], xnT_cur[:, ec * BLK + tsub * 128: ec * BLK + tsub * 128 + 128],
                        wvT[:, ec * 128:(ec + 1) * 128], start=(ec == 0), stop=False))
                fns.append(lambda e: e.matmul(
                    psV[:, 0:128], onesrow_b[0:1, 0:128], bv_sb[:], start=False, stop=True))
                P.op_group("tensor", fns, [xnT_cur, wvT, onesrow_b, bv_sb], [psV_k])
                kb_i = tb * 4 + tsub
                P.op("vector", lambda e, kb_i=kb_i: e.tensor_copy(V_sb[:, kb_i * 128:(kb_i + 1) * 128], psV[:, 0:128]), [psV_k], [V_sb])
            fns = []
            for ec in range(ECH):
                fns.append(lambda e, ec=ec: e.matmul(
                    psM[:, 0:BLK], wkT[:, ec * 128:(ec + 1) * 128], xnT_cur[:, ec * BLK:(ec + 1) * BLK],
                    start=(ec == 0), stop=False))
            fns.append(lambda e: e.matmul(psM[:, 0:BLK], bk_sb[:], onesrow_b[:], start=False, stop=True))
            P.op_group("tensor", fns, [xnT_cur, wkT, bk_sb, onesrow_b], [psM_k])
            rope_apply(psM[:, 0:BLK], psM_k, K_sb, K_sb, tb * BLK, (tb + 1) * BLK, rc, rs, 0)
            for hc in range(4):
                qps, qps_k = (psU, psU_k) if hc % 2 == 0 else (psQ2, psQ2_k)
                qs = qspill[hc]
                fns = []
                for ec in range(ECH):
                    fns.append(lambda e, ec=ec, hc=hc, qps=qps: e.matmul(
                        qps[:, 0:BLK], wqT[:, ec * 512 + hc * 128: ec * 512 + (hc + 1) * 128],
                        xnT_cur[:, ec * BLK:(ec + 1) * BLK], start=(ec == 0), stop=False))
                fns.append(lambda e, hc=hc, qps=qps: e.matmul(qps[:, 0:BLK], bq_sb[0:1, hc * 128:(hc + 1) * 128], onesrow_b[:], start=False, stop=True))
                P.op_group("tensor", fns, [xnT_cur, wqT, bq_sb, onesrow_b], [qps_k])
                rope_apply(qps[:, 0:BLK], qps_k, qs, qs, 0, BLK, rc, rs, hc % 2)
                P.dma("c_spill", out=qT_scr[hc * 128:(hc + 1) * 128, tb * BLK:(tb + 1) * BLK], in_=qs[:],
                      reads=[qs], writes=[f"qT#{hc}#{tb}"])
            norm_track(K_sb[:, tb * BLK:(tb + 1) * BLK], [K_sb], nmax_k)
            for hc in range(4):
                norm_track(qspill[hc][:], [qspill[hc]], nmax_q)
        for tb in range(4):
            for tsub in range(4):
                r0 = tb * BLK + tsub * 128
                xo = ln_block(x_own[r0:r0 + 128, :], tb * 4 + tsub)
                transpose_to(xo, xnT_own, lambda ec, tb=tb: ec * 2048 + tb * BLK, tsub)
        P.op("vector", lambda e: e.tensor_tensor(ntmp[:], nmax_q[:], nmax_k[:], ALU.mult), [nmax_q, nmax_k], [ntmp])
        P.op("scalar", lambda e: e.activation(ntmp[:], ntmp[:], AF.Sqrt, scale=SCALE * SCALE), [ntmp], [ntmp])
        P.op("vector", lambda e: e.tensor_scalar(ntmp[:], ntmp[:], -1.0, None, ALU.mult), [ntmp], [ntmp])
        P.op("tensor", lambda e: e.matmul(psR[:, 0:1], onesrow_f[0:1, :], ntmp[:], start=True, stop=True), [onesrow_f, ntmp], [psR_k])
        P.op("vector", lambda e: e.tensor_copy(negB[:], psR[:, 0:1]), [psR_k], [negB])

        if debug:
            P.dma("c_spill", out=dbg["d_kT"][:], in_=K_sb[:], reads=[K_sb])
            P.dma("c_spill", out=dbg["d_v"][:], in_=V_sb[:], reads=[V_sb])
            P.dma("c_spill", out=dbg["d_xnT"][:], in_=xnT_own[:], reads=[xnT_own])
            P.dma("c_spill", out=dbg["d_B"][:], in_=negB[:], reads=[negB])
            qv = dbg["d_qT"]
            for hc in range(4):
                for tb2 in range(NB):
                    P.dma("c_spill", out=qv[hc * 128:(hc + 1) * 128, tb2 * BLK:(tb2 + 1) * BLK],
                          in_=qT_scr[hc * 128:(hc + 1) * 128, tb2 * BLK:(tb2 + 1) * BLK],
                          reads=[f"qT#{hc}#{tb2}"], writes=[f"dq#{hc}#{tb2}"])

        P.barrier_all()
        free_group("p1")
        free_group("p1x")

    if phases >= 2:
        qh_buf = [sb(f"qh{i}", [128, BLK], BF16, "p2") for i in range(2)]
        pt_buf = [sb(f"pt{i}", [128, BLK], BF16, "p2") for i in range(4)]
        recip_sb = sb("recip_sb", [1, BLK], F32, "p2")
        rb_sb = sb("rb_sb", [128, BLK], F32, "p2")
        oh_sb = sb("oh_sb", [128, BLK], BF16, "p2")
        a2a_keys = []
        pending = []
        for hc in range(4):
            for qt in range(NB):
                qb = qh_buf[qt % 2]
                if qt == 0:
                    P.dma("c_q", out=qb[:], in_=qT_scr[hc * 128:(hc + 1) * 128, qt * BLK:(qt + 1) * BLK],
                          reads=[f"qT#{hc}#{qt}"], writes=[qb])
                nkc = (qt + 1) * 4
                if qt + 1 < NB:
                    nqb = qh_buf[(qt + 1) % 2]
                    P.dma("c_q", out=nqb[:], in_=qT_scr[hc * 128:(hc + 1) * 128, (qt + 1) * BLK:(qt + 2) * BLK],
                          reads=[f"qT#{hc}#{qt + 1}"], writes=[nqb])
                elif hc + 1 < 4:
                    nqb = qh_buf[0]
                    P.dma("c_q", out=nqb[:], in_=qT_scr[(hc + 1) * 128:(hc + 2) * 128, 0:BLK],
                          reads=[f"qT#{hc + 1}#0"], writes=[nqb])
                psO, psO_k = (PSF[4], PS[4]) if qt % 2 == 0 else (PSF[6], PS[6])
                psS_f, psS_k = (PSF[5], PS[5]) if qt % 2 == 0 else (PSF[7], PS[7])
                psS = psS_f[0:1, :]

                def emit_score(kb, qb=qb, qt=qt):
                    ps, psk = PSF[kb % 4], PS[kb % 4]
                    pt = pt_buf[kb % 4]
                    P.op("tensor", lambda e, kb=kb, ps=ps, qb=qb: e.matmul(
                        ps[:, 0:BLK], K_sb[:, kb * 128:(kb + 1) * 128], qb[:], start=True, stop=True),
                        [K_sb, qb], [psk])
                    P.op("scalar", lambda e, ps=ps, pt=pt: e.activation(pt[:], ps[:, 0:BLK], AF.Exp, bias=negB[:], scale=SCALE),
                         [psk, negB], [pt])
                    d = kb * 128 - qt * BLK
                    if d >= 0:
                        off = 384 - d
                        P.op("vector", lambda e, pt=pt, off=off: e.tensor_tensor(pt[:], pt[:], maskm[:, off:off + BLK], ALU.mult),
                             [pt, maskm], [pt])

                def emit_pv(kb, nkc=nkc, psO=psO, psS=psS, psO_k=psO_k, psS_k=psS_k):
                    pt = pt_buf[kb % 4]
                    fns = [
                        lambda e, kb=kb, pt=pt, nkc=nkc, psO=psO: e.matmul(
                            psO[:, 0:BLK], V_sb[:, kb * 128:(kb + 1) * 128], pt[:], start=(kb == 0), stop=(kb == nkc - 1)),
                        lambda e, kb=kb, pt=pt, nkc=nkc, psS=psS: e.matmul(
                            psS[0:1, 0:BLK], onescol_b[:], pt[:], start=(kb == 0), stop=(kb == nkc - 1)),
                    ]
                    P.op_group("tensor", fns, [V_sb, onescol_b, pt], [psO_k, psS_k])

                for kb0 in range(min(3, nkc)):
                    emit_score(kb0)
                for kb in range(nkc):
                    if kb + 3 < nkc:
                        emit_score(kb + 3)
                    emit_pv(kb)

                def division(hc=hc, qt=qt, psO=psO, psS=psS, psO_k=psO_k, psS_k=psS_k):
                    P.op("vector", lambda e, psS=psS: e.reciprocal(recip_sb[:], psS[0:1, 0:BLK]), [psS_k], [recip_sb])
                    P.op("tensor", lambda e: e.matmul(psR[:, 0:BLK], onesrow_f[:], recip_sb[:], start=True, stop=True),
                         [onesrow_f, recip_sb], [psR_k])
                    P.op("vector", lambda e: e.tensor_copy(rb_sb[:], psR[:, 0:BLK]), [psR_k], [rb_sb])
                    P.op("vector", lambda e, psO=psO: e.tensor_tensor(oh_sb[:], psO[:, 0:BLK], rb_sb[:], ALU.mult), [psO_k, rb_sb], [oh_sb])
                    qu, qc = qt // 4, qt % 4
                    for piece in (qu, qu + 4):
                        key = f"a2a#{piece}#{hc}#{qt}"
                        a2a_keys.append(key)
                        P.dma("c_spill", out=a2a_in[piece, hc * 128:(hc + 1) * 128, qc * BLK:(qc + 1) * BLK],
                              in_=oh_sb[:], reads=[oh_sb], writes=[key])

                pending.append(division)
                if len(pending) > 1:
                    pending.pop(0)()
        for fn in pending:
            fn()
        P.wait_keys("gpsimd", a2a_keys)
        _snap2 = P.eng["gpsimd"].count
        from concourse.aot_env import allow_ici_all_to_all

        def do_a2a(e):
            with allow_ici_all_to_all():
                return e.collective_compute(
                    "AllToAll", ALU.bypass, replica_groups=[list(range(8))],
                    ins=[a2a_in[:].opt()], outs=[a2a_out[:].opt()])

        P.op("gpsimd", do_a2a, a2a_keys, [a2a_out])
        if debug:
            P.dma("c_spill", out=dbg["d_a2a"][:], in_=a2a_out[:], reads=[a2a_out])
        P.barrier_all(limit={"gpsimd": _snap2})
        free_group("p2")
        free_group("p12")

    if phases >= 3:
        oTa = [sb(f"oTa{i}", [128, BLK], BF16, "p3") for i in range(2)]
        oTb = [sb(f"oTb{i}", [128, BLK], BF16, "p3") for i in range(2)]
        oT_qt = sb("oT_qt", [128, 16 * BLK], BF16, "p3")
        woT_buf = [sb(f"woTb{i}", [128, 16 * 128], BF16, "p3") for i in range(2)]
        wgu_buf = [sb(f"wgu{i}", [128, 32 * 128], BF16, "p3") for i in range(2)]
        wd_buf = [sb(f"wdb{i}", [128, 16 * 256], BF16, "p3") for i in range(2)]
        inter = [sb(f"inter{i}", [128, BLK], BF16, "p3") for i in range(16)]
        gelu_s = [sb(f"gelu_s{i}", [128, BLK], F32, "p3") for i in range(2)]
        oTt = gelu_s[0]
        acc = sb("acc", [128, 16 * BLK], F32, "p3")
        xo_tile = sb("xo0", [128, EE], F32, "p3")
        out_sb = sb("outsb0", [128, EE], F32, "p3")

        woT_v2 = woT_scr[:].rearrange("(oc p c) -> oc p c", oc=16, p=128)
        for qt in range(4):
            def do_merge(qt=qt):
                for hd in range(16):
                    rr, sub = hd // 4, hd % 4
                    ta = oTa[hd % 2]
                    tb_ = oTb[hd % 2]
                    P.dma("c_oT", out=ta[:], in_=a2a_out[rr, sub * 128:(sub + 1) * 128, qt * BLK:(qt + 1) * BLK],
                          reads=[a2a_out], writes=[ta])
                    P.dma("c_oT", out=tb_[:], in_=a2a_out[rr + 4, sub * 128:(sub + 1) * 128, qt * BLK:(qt + 1) * BLK],
                          reads=[a2a_out], writes=[tb_])
                    P.op("vector", lambda e, ta=ta: e.tensor_scalar(oTt[:], ta[:], sel0[:], None, ALU.mult), [ta, sel0], [oTt])
                    P.op("vector", lambda e, tb_=tb_, hd=hd: e.scalar_tensor_tensor(
                        oT_qt[:, hd * BLK:(hd + 1) * BLK], tb_[:], sel1[:], oTt[:], ALU.mult, ALU.add),
                        [tb_, sel1, oTt], [oT_qt])

            if qt > 0:
                do_merge()
            for grp in range(8):
                for nch in range(16):
                    wgb = wgu_buf[nch % 2]
                    gps, gpk = (psM, psM_k) if nch % 2 == 0 else (psR, psR_k)
                    ups, upk = (psU, psU_k) if nch % 2 == 0 else (psO, psO_k)
                    src_gu = ffn_full[grp * SH + nch * 524288: grp * SH + (nch + 1) * 524288]
                    P.dma("c_fw", out=wgb[:].rearrange("p (gu c) -> p gu c", gu=2),
                          in_=src_gu.rearrange("(gu p c) -> p gu c", gu=2, p=128), reads=[ffn_full], writes=[wgb])
                    fns = []
                    for ec in range(ECH):
                        fns.append(lambda e, ec=ec, wgb=wgb, qt=qt, gps=gps: e.matmul(
                            gps[:, 0:BLK], wgb[:, ec * 128:(ec + 1) * 128], xnT_own[:, ec * 2048 + qt * BLK: ec * 2048 + (qt + 1) * BLK],
                            start=(ec == 0), stop=(ec == ECH - 1)))
                    P.op_group("tensor", fns, [wgb, xnT_own], [gpk])
                    fns = []
                    for ec in range(ECH):
                        fns.append(lambda e, ec=ec, wgb=wgb, qt=qt, ups=ups: e.matmul(
                            ups[:, 0:BLK], wgb[:, 2048 + ec * 128:(ec + 1) * 128 + 2048], xnT_own[:, ec * 2048 + qt * BLK: ec * 2048 + (qt + 1) * BLK],
                            start=(ec == 0), stop=(ec == ECH - 1)))
                    P.op_group("tensor", fns, [wgb, xnT_own], [upk])
                    it = inter[nch]
                    gl = gelu_s[nch % 2]
                    P.op("scalar", lambda e, gps=gps, gl=gl: e.activation(gl[:], gps[:, 0:BLK], AF.Gelu_apprx_tanh), [gpk], [gl])
                    P.op("vector", lambda e, it=it, ups=ups, gl=gl: e.tensor_tensor(it[:], gl[:], ups[:, 0:BLK], ALU.mult), [gl, upk], [it])
                if qt == 0 and grp == 3:
                    do_merge()
                for sh in range(8):
                    wdb = wd_buf[sh % 2]
                    src = ffn_full[sh * SH + 2 * SH_G + grp * 524288: sh * SH + 2 * SH_G + (grp + 1) * 524288]
                    P.dma("c_fd", out=wdb[:], in_=src.rearrange("(p c) -> p c", p=128), reads=[ffn_full], writes=[wdb])
                    for hcl in range(2):
                        hc_g = sh * 2 + hcl
                        ps, pskey = (psA_f, psA) if hcl % 2 == 0 else (psB_f, psB)
                        fns = []
                        for ncl in range(16):
                            fns.append(lambda e, ncl=ncl, hcl=hcl, ps=ps, wdb=wdb: e.matmul(
                                ps[:, 0:BLK], wdb[:, ncl * 256 + hcl * 128: ncl * 256 + (hcl + 1) * 128],
                                inter[ncl][:], start=(ncl == 0), stop=(ncl == 15)))
                        P.op_group("tensor", fns, [wdb] + [inter[ncl] for ncl in range(16)], [pskey])
                        if grp == 0:
                            P.op("vector", lambda e, hc_g=hc_g, ps=ps: e.tensor_copy(
                                acc[:, hc_g * BLK:(hc_g + 1) * BLK], ps[:, 0:BLK]),
                                [pskey], [f"acc#{hc_g}"])
                        else:
                            P.op("vector", lambda e, hc_g=hc_g, ps=ps: e.tensor_tensor(
                                acc[:, hc_g * BLK:(hc_g + 1) * BLK], acc[:, hc_g * BLK:(hc_g + 1) * BLK], ps[:, 0:BLK], ALU.add),
                                [pskey, f"acc#{hc_g}"], [f"acc#{hc_g}"])
            for oc in range(16):
                wob = woT_buf[oc % 2]
                P.dma("c_oT", out=wob[:], in_=woT_v2[oc], reads=[f"woT#{oc}"], writes=[wob])
                ps, pskey = (psA_f, psA) if oc % 2 == 0 else (psB_f, psB)
                fns = []
                for hd in range(16):
                    fns.append(lambda e, hd=hd, ps=ps, wob=wob: e.matmul(
                        ps[:, 0:BLK], wob[:, hd * 128:(hd + 1) * 128], oT_qt[:, hd * BLK:(hd + 1) * BLK],
                        start=(hd == 0), stop=False))
                fns.append(lambda e, oc=oc, ps=ps: e.matmul(
                    ps[:, 0:BLK], bo_sb[0:1, oc * 128:(oc + 1) * 128], onesrow_b[:], start=False, stop=True))
                P.op_group("tensor", fns, [wob, oT_qt, bo_sb, onesrow_b], [pskey])
                P.op("vector", lambda e, oc=oc, ps=ps: e.tensor_tensor(
                    acc[:, oc * BLK:(oc + 1) * BLK], acc[:, oc * BLK:(oc + 1) * BLK], ps[:, 0:BLK], ALU.add),
                    [pskey, f"acc#{oc}"], [f"acc#{oc}"])
            for tsub in range(4):
                r0 = qt * BLK + tsub * 128
                P.dma("c_xo", out=xo_tile[:], in_=x_own[r0:r0 + 128, :], writes=[xo_tile])
                for oc in range(16):
                    trp, trk = (psM, psM_k) if oc % 2 == 0 else (psU, psU_k)
                    P.op("tensor", lambda e, oc=oc, tsub=tsub, trp=trp: e.transpose(
                        trp[:, 0:128], acc[:, oc * BLK + tsub * 128: oc * BLK + (tsub + 1) * 128], identf[:]),
                        [f"acc#{oc}", identf], [trk], pe_relaxed=False)
                    P.op("vector", lambda e, oc=oc, trp=trp: e.tensor_tensor(
                        out_sb[:, oc * 128:(oc + 1) * 128], trp[:, 0:128], xo_tile[:, oc * 128:(oc + 1) * 128], ALU.add),
                        [trk, xo_tile], [out_sb])
                P.dma("c_out", out=out_t[r0:r0 + 128, :], in_=out_sb[:], reads=[out_sb], writes=[f"out#{qt}#{tsub}"])

        P.wait_tokens("gpsimd", [P.barrier_token("c_out")])
    if debug:
        P.wait_tokens("gpsimd", [P.barrier_token("c_spill")])

    P.finalize()
    for h in reversed(alloc_order):
        h.__exit__(None, None, None)
    for h in reversed(groups.pop("psum", [])):
        h.__exit__(None, None, None)
    return nc



def _host_constants():
    identb = np.eye(128, dtype=np.float32).astype(ml_dtypes.bfloat16)
    identf = np.eye(128, dtype=np.float32)
    inv_freq = (1.0 / 10000.0 ** (np.arange(0, DH, 2, dtype=np.float32) / DH)).astype(np.float32)
    t = np.arange(SS, dtype=np.float32)
    ang = np.outer(inv_freq, t)
    ropec = np.cos(ang).astype(np.float32)
    ropes = np.sin(ang).astype(np.float32)
    i = np.arange(128)[:, None]
    u = np.arange(896)[None, :]
    maskm = ((u - i) >= 384).astype(np.float32).astype(ml_dtypes.bfloat16)
    return identb, identf, ropec, ropes, maskm


_CACHE = {}


def kernel(L0_attn_x, L0_attn_wq, L0_attn_bq, L0_attn_wk, L0_attn_bk,
           L0_attn_wv, L0_attn_bv, L0_attn_wo, L0_attn_bo,
           L0_ffn_gate_W_packed, L0_ffn_gate_scales,
           L0_ffn_up_W_packed, L0_ffn_up_scales,
           L0_ffn_down_W_packed, L0_ffn_down_scales,
           _debug=0, _phases=3):
    x = np.ascontiguousarray(np.asarray(L0_attn_x, dtype=np.float32))
    wq = np.asarray(L0_attn_wq, dtype=np.float32)
    bq = np.asarray(L0_attn_bq, dtype=np.float32)
    wk = np.asarray(L0_attn_wk, dtype=np.float32)
    bk = np.asarray(L0_attn_bk, dtype=np.float32)
    wv = np.asarray(L0_attn_wv, dtype=np.float32)
    bv = np.asarray(L0_attn_bv, dtype=np.float32)
    wo = np.asarray(L0_attn_wo, dtype=np.float32)
    bo = np.asarray(L0_attn_bo, dtype=np.float32)
    gpk = np.asarray(L0_ffn_gate_W_packed, dtype=np.int32)
    gsc = np.asarray(L0_ffn_gate_scales, dtype=np.float32)
    upk = np.asarray(L0_ffn_up_W_packed, dtype=np.int32)
    usc = np.asarray(L0_ffn_up_scales, dtype=np.float32)
    dpk = np.asarray(L0_ffn_down_W_packed, dtype=np.int32)
    dsc = np.asarray(L0_ffn_down_scales, dtype=np.float32)

    identb, identf, ropec, ropes, maskm = _host_constants()

    key = (_debug, _phases)
    if key not in _CACHE:
        _CACHE[key] = build(debug=_debug, phases=_phases)
    nc = _CACHE[key]

    in_maps = []
    for c in range(8):
        b, g = c // 4, c % 4
        sel0 = np.full((128, 1), 1.0 if b == 0 else 0.0, np.float32)
        in_maps.append(dict(
            x_full=x[b], x_own=np.ascontiguousarray(x[b, g * 2048:(g + 1) * 2048]),
            wq_g=np.ascontiguousarray(wq[g * 512:(g + 1) * 512]),
            bq_g=np.ascontiguousarray(bq[g * 512:(g + 1) * 512]).reshape(1, 512),
            wk_g=np.ascontiguousarray(wk[g * 128:(g + 1) * 128]),
            bk_g=np.ascontiguousarray(bk[g * 128:(g + 1) * 128]).reshape(1, 128),
            wv_g=np.ascontiguousarray(wv[g * 128:(g + 1) * 128]),
            bv_g=np.ascontiguousarray(bv[g * 128:(g + 1) * 128]).reshape(1, 128),
            wo=wo, bo=bo.reshape(1, EE),
            gate_pk=np.ascontiguousarray(gpk[c * GSH:(c + 1) * GSH]),
            gate_sc=np.ascontiguousarray(gsc[c * GSH:(c + 1) * GSH]),
            up_pk=np.ascontiguousarray(upk[c * GSH:(c + 1) * GSH]),
            up_sc=np.ascontiguousarray(usc[c * GSH:(c + 1) * GSH]),
            down_pk=np.ascontiguousarray(dpk[c * DSH:(c + 1) * DSH]),
            down_sc=np.ascontiguousarray(dsc[c * DSH:(c + 1) * DSH]),
            identb=identb, identf=identf, ropec=ropec, ropes=ropes,
            maskm=maskm, sel0=sel0, sel1=(1.0 - sel0).astype(np.float32),
        ))

    res = run_bass_kernel_spmd(nc, in_maps, core_ids=list(range(8)))
    if _debug:
        return res
    out = np.empty((BB, SS, EE), np.float32)
    for c in range(8):
        b, g = c // 4, c % 4
        out[b, g * 2048:(g + 1) * 2048] = res.results[c]["out"]
    return out


# version 23
# speedup vs baseline: 1.0365x; 1.0365x over previous
import numpy as np
import ml_dtypes
from collections import defaultdict

import concourse.bass as bass
import concourse.mybir as mybir
from concourse.bass_utils import run_bass_kernel_spmd

F32 = mybir.dt.float32
BF16 = mybir.dt.bfloat16
I32 = mybir.dt.int32
I16 = mybir.dt.int16
AF = mybir.ActivationFunctionType
ALU = mybir.AluOpType
AX = mybir.AxisListType

BB, SS, EE = 2, 8192, 2048
H, HKV, DH = 16, 4, 128
NMLP = 16384
BLK = 512
NB = SS // BLK
ECH = EE // 128
SCALE = DH ** -0.5
GSH = 2048
DSH = 256
SH_G = 2048 * 2048
SH_D = 16384 * 256
SH = SH_G * 2 + SH_D

ENGINES = ("sync", "scalar", "vector", "tensor", "gpsimd")


def _key_of(x):
    if isinstance(x, str):
        return x
    t = getattr(x, "tensor", None)
    if t is not None:
        return t.name
    return x.name


class _Chan:
    __slots__ = ("name", "queue", "sem", "count", "mode", "key")

    def __init__(self, nc, name, queue, mode):
        self.name = name
        self.queue = queue
        self.sem = nc.alloc_semaphore(name=f"ch_{name}")
        self.count = 0
        self.mode = mode
        self.key = None


class _Tok:
    __slots__ = ("eng", "sem", "val", "chan")

    def __init__(self, eng, sem=None, val=0, chan=None):
        self.eng = eng
        self.sem = sem
        self.val = val
        self.chan = chan

    def resolve(self):
        if self.chan is not None:
            if self.chan.mode == "bulk":
                return self.chan.sem, 16 * self.chan.count
            return self.chan.sem, 16 * self.val
        return self.sem, self.val


class _Eng:
    def __init__(self, nc, name):
        self.name = name
        self.sem = nc.alloc_semaphore(name=f"sem_{name}")
        self.count = 0
        self.waited = defaultdict(int)
        self.items = []


class Prog:
    def __init__(self, nc):
        self.nc = nc
        self.eng = {n: _Eng(nc, n) for n in ENGINES}
        self.chans = {}
        self.last_w = {}
        self.readers = defaultdict(list)
        self.finalized = False

    def chan(self, name, queue, mode="ring"):
        c = _Chan(self.nc, name, queue, mode)
        self.chans[name] = c
        return c

    def _wait(self, e, tok, waits):
        sem, val = tok.resolve()
        if e.waited[id(sem)] < val:
            waits.append((sem, val))
            e.waited[id(sem)] = val

    def _deps(self, engname, e, rkeys, wkeys, pe_relaxed):
        waits = []
        for k in rkeys:
            tok = self.last_w.get(k)
            if tok is not None and not (pe_relaxed and tok.eng == engname == "tensor"):
                self._wait(e, tok, waits)
        for k in wkeys:
            tok = self.last_w.get(k)
            if tok is not None and not (pe_relaxed and tok.eng == engname == "tensor"):
                self._wait(e, tok, waits)
            for rtok in self.readers.get(k, ()):
                if pe_relaxed and rtok.eng == engname == "tensor":
                    continue
                self._wait(e, rtok, waits)
        return waits

    def _record(self, tok, rkeys, wkeys):
        for k in wkeys:
            self.last_w[k] = tok
            self.readers[k] = []
        for k in rkeys:
            self.readers[k].append(tok)

    def op(self, engname, fn, reads=None, writes=None, pe_relaxed=True):
        return self.op_group(engname, [fn], reads, writes, pe_relaxed)

    def op_group(self, engname, fns, reads=None, writes=None, pe_relaxed=True):
        assert not self.finalized
        e = self.eng[engname]
        rkeys = [_key_of(r) for r in (reads or [])]
        wkeys = [_key_of(w) for w in (writes or [])]
        waits = self._deps(engname, e, rkeys, wkeys, pe_relaxed)
        e.count += 1
        tok = _Tok(engname, sem=e.sem, val=e.count)
        self._record(tok, rkeys, wkeys)
        n = len(fns)
        for i, fn in enumerate(fns):
            if i == n - 1:
                e.items.append((fn, waits if i == 0 else (), e.sem, 1))
            else:
                e.items.append((fn, waits if i == 0 else (), None, 0))
        return tok

    def dma(self, ch, out, in_, reads=None, writes=None, **kw):
        assert not self.finalized
        if isinstance(ch, str):
            ch = self.chans[ch]
        e = self.eng[ch.queue]
        rkeys = [_key_of(r) for r in (reads if reads is not None else [in_])]
        wkeys = [_key_of(w) for w in (writes if writes is not None else [out])]
        if ch.mode == "ring":
            k = tuple(sorted(wkeys))
            if ch.key is None:
                ch.key = k
            else:
                assert ch.key == k, f"ring chan {ch.name}: {k} != {ch.key}"
        waits = self._deps(ch.queue, e, rkeys, wkeys, pe_relaxed=False)
        ch.count += 1
        tok = _Tok(ch.queue, chan=ch, val=ch.count)
        self._record(tok, rkeys, wkeys)

        def fn(eng, _o=out, _i=in_, _kw=kw):
            return eng.dma_start(out=_o, in_=_i, **_kw)

        e.items.append((fn, waits, ch.sem, 16))
        return tok

    def wait_tokens(self, engname, tokens):
        e = self.eng[engname]
        waits = []
        for tok in tokens:
            self._wait(e, tok, waits)
        if waits:
            e.items.append((None, waits, None, 0))

    def wait_keys(self, engname, keys):
        toks = [self.last_w[_key_of(k)] for k in keys if _key_of(k) in self.last_w]
        self.wait_tokens(engname, toks)

    def barrier_all(self, limit=None):
        toks = []
        for n in ENGINES:
            e = self.eng[n]
            cnt = min(e.count, limit.get(n, e.count)) if limit else e.count
            if cnt:
                toks.append(_Tok(n, sem=e.sem, val=cnt))
        for c in self.chans.values():
            if c.count:
                toks.append(_Tok(c.queue, chan=c, val=c.count))
        for n in ENGINES:
            self.wait_tokens(n, toks)

    def barrier_token(self, ch):
        if isinstance(ch, str):
            ch = self.chans[ch]
        return _Tok(ch.queue, chan=ch, val=ch.count)

    def finalize(self):
        assert not self.finalized
        self.finalized = True
        block = self.nc.Block()
        block.__enter__()
        for name in ENGINES:
            e = self.eng[name]
            if not e.items:
                continue
            section = getattr(block, name)

            def emit(eng, _e=e):
                for fn, waits, sem, amt in _e.items:
                    for wsem, wval in waits:
                        eng.wait_ge(wsem, wval)
                    if fn is not None:
                        inst = fn(eng)
                        if sem is not None:
                            inst.then_inc(sem, amt)

            section(emit)
        block.__exit__(None, None, None)



def build(debug=0, phases=3):
    nc = bass.Bass("TRN2", target_bir_lowering=False)

    x_full = nc.dram_tensor("x_full", [SS, EE], F32, kind="ExternalInput")
    x_own = nc.dram_tensor("x_own", [2048, EE], F32, kind="ExternalInput")
    wq_g = nc.dram_tensor("wq_g", [512, EE], F32, kind="ExternalInput")
    bq_g = nc.dram_tensor("bq_g", [1, 512], F32, kind="ExternalInput")
    wk_g = nc.dram_tensor("wk_g", [128, EE], F32, kind="ExternalInput")
    bk_g = nc.dram_tensor("bk_g", [1, 128], F32, kind="ExternalInput")
    wv_g = nc.dram_tensor("wv_g", [128, EE], F32, kind="ExternalInput")
    bv_g = nc.dram_tensor("bv_g", [1, 128], F32, kind="ExternalInput")
    wo_t = nc.dram_tensor("wo", [EE, EE], F32, kind="ExternalInput")
    bo_t = nc.dram_tensor("bo", [1, EE], F32, kind="ExternalInput")
    gate_pk = nc.dram_tensor("gate_pk", [GSH, 1024], I32, kind="ExternalInput")
    gate_sc = nc.dram_tensor("gate_sc", [GSH, 64], F32, kind="ExternalInput")
    up_pk = nc.dram_tensor("up_pk", [GSH, 1024], I32, kind="ExternalInput")
    up_sc = nc.dram_tensor("up_sc", [GSH, 64], F32, kind="ExternalInput")
    down_pk = nc.dram_tensor("down_pk", [DSH, 8192], I32, kind="ExternalInput")
    down_sc = nc.dram_tensor("down_sc", [DSH, 512], F32, kind="ExternalInput")
    identb_t = nc.dram_tensor("identb", [128, 128], BF16, kind="ExternalInput")
    identf_t = nc.dram_tensor("identf", [128, 128], F32, kind="ExternalInput")
    ropec_t = nc.dram_tensor("ropec", [64, SS], F32, kind="ExternalInput")
    ropes_t = nc.dram_tensor("ropes", [64, SS], F32, kind="ExternalInput")
    maskm_t = nc.dram_tensor("maskm", [128, 896], BF16, kind="ExternalInput")
    sel0_t = nc.dram_tensor("sel0", [128, 1], F32, kind="ExternalInput")
    sel1_t = nc.dram_tensor("sel1", [128, 1], F32, kind="ExternalInput")

    out_t = nc.dram_tensor("out", [2048, EE], F32, kind="ExternalOutput")

    ffn_shard = nc.dram_tensor("ffn_shard", [SH], BF16)
    ffn_full = nc.dram_tensor("ffn_full", [8 * SH], BF16)
    woT_scr = nc.dram_tensor("woT_scr", [16 * 16 * 128 * 128], BF16)
    qT_scr = nc.dram_tensor("qT_scr", [512, SS], BF16)
    a2a_in = nc.dram_tensor("a2a_in", [8, 512, 2048], BF16)
    a2a_out = nc.dram_tensor("a2a_out", [8, 512, 2048], BF16)

    dbg = {}
    if debug:
        dbg["d_wqT"] = nc.dram_tensor("d_wqT", [128, 16 * 512], BF16, kind="ExternalOutput")
        dbg["d_ffn"] = nc.dram_tensor("d_ffn", [8 * SH], BF16, kind="ExternalOutput")
        dbg["d_woT"] = nc.dram_tensor("d_woT", [16 * 16 * 128 * 128], BF16, kind="ExternalOutput")
        if phases >= 1:
            dbg["d_kT"] = nc.dram_tensor("d_kT", [128, SS], BF16, kind="ExternalOutput")
            dbg["d_v"] = nc.dram_tensor("d_v", [128, SS], BF16, kind="ExternalOutput")
            dbg["d_qT"] = nc.dram_tensor("d_qT", [512, SS], BF16, kind="ExternalOutput")
            dbg["d_xnT"] = nc.dram_tensor("d_xnT", [128, ECH * 2048], BF16, kind="ExternalOutput")
            dbg["d_B"] = nc.dram_tensor("d_B", [128, 1], F32, kind="ExternalOutput")
        if phases >= 2:
            dbg["d_a2a"] = nc.dram_tensor("d_a2a", [8, 512, 2048], BF16, kind="ExternalOutput")

    P = Prog(nc)
    groups = defaultdict(list)
    alloc_order = []

    def sb(name, shape, dtype, grp):
        h = nc.sbuf_tensor(name, shape, dtype)
        t = h.__enter__()
        groups[grp].append(h)
        alloc_order.append(h)
        return t

    def free_group(grp):
        for h in reversed(groups.pop(grp, [])):
            h.__exit__(None, None, None)
            alloc_order.remove(h)

    def psum(name, shape, dtype):
        h = nc.psum_tensor(name, shape, dtype)
        t = h.__enter__()
        groups["psum"].append(h)
        return t

    identb = sb("identb_s", [128, 128], BF16, "persist")
    identf = sb("identf_s", [128, 128], F32, "persist")
    maskm = sb("maskm_s", [128, 896], BF16, "persist")
    onescol_b = sb("onescol_b", [128, 1], BF16, "persist")
    onesrow_b = sb("onesrow_b", [1, BLK], BF16, "persist")
    onesrow_f = sb("onesrow_f", [1, 128], F32, "persist")

    bq_sb = sb("bq_sb", [1, 512], BF16, "persist")
    bk_sb = sb("bk_sb", [1, 128], BF16, "persist")
    bv_sb = sb("bv_sb", [1, 128], BF16, "persist")
    bo_sb = sb("bo_sb", [1, EE], BF16, "persist")
    sel0 = sb("sel0_s", [128, 1], F32, "persist")
    sel1 = sb("sel1_s", [128, 1], F32, "persist")
    negB = sb("negB", [128, 1], F32, "persist")
    xnT_own = sb("xnT_own", [128, ECH * 2048], BF16, "persist")

    K_sb = sb("K_sb", [128, SS], BF16, "p12")
    V_sb = sb("V_sb", [128, SS], BF16, "p12")

    PS = [psum(f"ps{i}", [128, 1024], BF16) for i in range(8)]
    PSF = [p[:].bitcast(F32) for p in PS]
    psA, psB = PS[6], PS[7]
    psA_f, psB_f = PSF[6], PSF[7]
    psM, psM_k = PSF[0], PS[0]
    psU, psU_k = PSF[1], PS[1]
    psO, psO_k = PSF[4], PS[4]
    psS, psS_k = PSF[5][0:1, :], PS[5]
    psR, psR_k = PSF[3], PS[3]

    P.chan("c_const", "sync", "bulk")
    P.chan("c_x", "sync", "bulk")
    P.chan("c_w", "sync", "bulk")
    P.chan("c_rope", "sync", "bulk")
    P.chan("c_spill", "gpsimd", "bulk")
    P.chan("c_q", "gpsimd", "bulk")
    P.chan("c_oT", "gpsimd", "bulk")
    P.chan("c_fw", "sync", "bulk")
    P.chan("c_fd", "sync", "bulk")
    P.chan("c_xo", "gpsimd", "bulk")
    P.chan("c_out", "gpsimd", "bulk")

    P.dma("c_const", out=identb[:], in_=identb_t[:])
    P.dma("c_const", out=identf[:], in_=identf_t[:])
    P.dma("c_const", out=maskm[:], in_=maskm_t[:])
    P.dma("c_const", out=sel0[:], in_=sel0_t[:])
    P.dma("c_const", out=sel1[:], in_=sel1_t[:])
    P.op("vector", lambda e: e.memset(onescol_b[:], 1.0), [], [onescol_b])
    P.op("vector", lambda e: e.memset(onesrow_b[:], 1.0), [], [onesrow_b])
    P.op("vector", lambda e: e.memset(onesrow_f[:], 1.0), [], [onesrow_f])
    for dst, srct in [(bq_sb, bq_g), (bk_sb, bk_g), (bv_sb, bv_g), (bo_sb, bo_t)]:
        P.dma("c_spill", out=dst[:], in_=srct[:])

    wqT = sb("wqT", [128, ECH * 512], BF16, "p1x")
    wkT = sb("wkT", [128, ECH * 128], BF16, "p1x")
    wvT = sb("wvT", [128, ECH * 128], BF16, "p1x")
    wbuf = [sb(f"wbuf{i}", [128, EE], F32, "p0") for i in range(2)]
    wbufb = sb("wbufb", [128, EE], BF16, "p0")
    pkb = [sb(f"pkb{i}", [128, 1024], I32, "p0") for i in range(2)]
    scb = [sb(f"scb{i}", [128, 64], F32, "p0") for i in range(2)]
    scbb = sb("scbb", [128, 64], BF16, "p0")
    dqw = [sb(f"dqw{i}", [128, 1024], I16, "p0") for i in range(2)]
    dqlo = [sb(f"dqlo{i}", [128, 1024], I16, "p0") for i in range(2)]
    dqhi = [sb(f"dqhi{i}", [128, 1024], I16, "p0") for i in range(2)]
    dqnib = [sb(f"dqnib{i}", [128, 2048], I16, "p0") for i in range(2)]
    dqm = sb("dqm", [128, 2048], I16, "p0")
    dqsgn = sb("dqsgn", [128, 2048], I16, "p0")
    dqge = sb("dqge", [128, 2048], I16, "p0")
    dqbits = sb("dqbits", [128, 2048], I16, "p0")
    dqval = sb("dqval", [128, 2048], BF16, "p0")
    asm = [sb(f"asm{i}", [128, 2048], BF16, "p0") for i in range(2)]

    def prep_wT(src, rows, dst):
        for t in range(rows // 128):
            wb = wbuf[t % 2]
            P.dma("c_w", out=wb[:], in_=src[t * 128:(t + 1) * 128, :], writes=[wb])
            P.op("vector", lambda e, wb=wb: e.tensor_copy(wbufb[:], wb[:]), [wb], [wbufb])
            for ec in range(ECH):
                ps = psA if ec % 2 == 0 else psB
                P.op("tensor", lambda e, ec=ec, ps=ps: e.transpose(ps[:, 0:128], wbufb[:, ec * 128:(ec + 1) * 128], identb[:]),
                     [wbufb, identb], [ps])
                c0 = ec * rows + t * 128
                P.op("vector", lambda e, ps=ps, c0=c0, dst=dst: e.tensor_copy(dst[:, c0:c0 + 128], ps[:, 0:128]), [ps], [dst])

    prep_wT(wq_g, 512, wqT)
    prep_wT(wk_g, 128, wkT)
    prep_wT(wv_g, 128, wvT)

    woT_v = woT_scr[:].rearrange("(oc p c) -> oc p c", oc=16, p=128)
    for t in range(16):
        wb = wbuf[t % 2]
        am = asm[t % 2]
        P.dma("c_w", out=wb[:], in_=wo_t[t * 128:(t + 1) * 128, :], writes=[wb])
        P.op("vector", lambda e, wb=wb: e.tensor_copy(wbufb[:], wb[:]), [wb], [wbufb])
        for ic in range(16):
            ps = psA if ic % 2 == 0 else psB
            P.op("tensor", lambda e, ic=ic, ps=ps: e.transpose(ps[:, 0:128], wbufb[:, ic * 128:(ic + 1) * 128], identb[:]),
                 [wbufb, identb], [ps])
            P.op("vector", lambda e, ic=ic, ps=ps, am=am: e.tensor_copy(am[:, ic * 128:(ic + 1) * 128], ps[:, 0:128]), [ps], [am])
        P.dma("c_spill", out=woT_v[t], in_=am[:], reads=[am], writes=[f"woT#{t}"])

    shard_keys = []

    def dequant_tile(pk_src, sc_src, rt, piece_dst):
        pb = pkb[rt % 2]
        scf = scb[rt % 2]
        P.dma("c_w", out=pb[:], in_=pk_src, writes=[pb])
        P.dma("c_w", out=scf[:], in_=sc_src, writes=[scf])
        dw, dl, dh, dn = dqw[rt % 2], dqlo[rt % 2], dqhi[rt % 2], dqnib[rt % 2]
        P.op("vector", lambda e, pb=pb, dw=dw: e.tensor_copy(dw[:], pb[:]), [pb], [dw])
        P.op("vector", lambda e, dw=dw, dl=dl: e.tensor_scalar(dl[:], dw[:], 15, None, ALU.bitwise_and), [dw], [dl])
        P.op("vector", lambda e, dw=dw, dh=dh: e.tensor_scalar(dh[:], dw[:], 4, 15, ALU.logical_shift_right, ALU.bitwise_and), [dw], [dh])
        nib_v = dn[:].rearrange("p (i two) -> p i two", two=2)
        P.op("vector", lambda e, nib_v=nib_v, dl=dl: e.tensor_copy(nib_v[:, :, 0], dl[:]), [dl], [dn])
        P.op("vector", lambda e, nib_v=nib_v, dh=dh: e.tensor_copy(nib_v[:, :, 1], dh[:]), [dh], [dn])
        P.op("vector", lambda e, dn=dn: e.tensor_scalar(dqm[:], dn[:], 7, None, ALU.bitwise_and), [dn], [dqm])
        P.op("vector", lambda e, dn=dn: e.tensor_scalar(dqsgn[:], dn[:], 8, 12, ALU.bitwise_and, ALU.logical_shift_left), [dn], [dqsgn])
        P.op("vector", lambda e: e.tensor_scalar(dqbits[:], dqm[:], 16128, None, ALU.mult), [dqm], [dqbits])
        P.op("vector", lambda e, dn=dn: e.tensor_scalar(dn[:], dqm[:], 64, 16128, ALU.mult, ALU.add), [dqm], [dn])
        P.op("vector", lambda e: e.tensor_scalar(dqge[:], dqm[:], 2, None, ALU.is_ge), [dqm], [dqge])
        P.op("vector", lambda e, dn=dn: e.copy_predicated(dqbits[:], dqge[:], dn[:]), [dqge, dn, dqbits], [dqbits])
        P.op("vector", lambda e: e.tensor_tensor(dqbits[:], dqbits[:], dqsgn[:], ALU.bitwise_or), [dqbits, dqsgn], [dqbits])
        P.op("vector", lambda e, scf=scf: e.tensor_copy(scbb[:], scf[:]), [scf], [scbb])
        sclb = scbb[:].unsqueeze(-1).broadcast_to([128, 64, 32])
        vv = dqbits[:].bitcast(BF16).rearrange("p (i b) -> p i b", b=32)
        ov = dqval[:].rearrange("p (i b) -> p i b", b=32)
        P.op("vector", lambda e, vv=vv, ov=ov, sclb=sclb: e.tensor_tensor(ov, vv, sclb, ALU.mult), [dqbits, scbb], [dqval])
        am = asm[rt % 2]
        for sub in range(16):
            ps = psA if sub % 2 == 0 else psB
            P.op("tensor", lambda e, sub=sub, ps=ps: e.transpose(ps[:, 0:128], dqval[:, sub * 128:(sub + 1) * 128], identb[:]),
                 [dqval, identb], [ps])
            P.op("vector", lambda e, sub=sub, ps=ps, am=am: e.tensor_copy(am[:, sub * 128:(sub + 1) * 128], ps[:, 0:128]), [ps], [am])
        dst, key = piece_dst(-1)
        shard_keys.append(key)
        src_ap = am[:] if dst.ndim == 2 else am[:].rearrange("p (ncl h) -> p ncl h", ncl=16)
        P.dma("c_spill", out=dst, in_=src_ap, reads=[am], writes=[key])

    for gu, pk_t, sc_t in [(0, gate_pk, gate_sc), (1, up_pk, up_sc)]:
        for nt in range(16):
            def gdst(sub, nt=nt, gu=gu):
                off = nt * 524288 + gu * 262144
                blk = ffn_shard[off: off + 262144]
                return (blk.rearrange("(p c) -> p c", p=128), f"ffs#{gu}#{nt}")

            dequant_tile(pk_t[nt * 128:(nt + 1) * 128, :], sc_t[nt * 128:(nt + 1) * 128, :], nt + 16 * gu, gdst)

    for ht in range(2):
        for cp in range(8):
            def ddst(sub, ht=ht, cp=cp):
                blk = ffn_shard[2 * SH_G + cp * 524288: 2 * SH_G + (cp + 1) * 524288]
                v = blk.rearrange("(p ncl h) -> p ncl h", p=128, ncl=16)
                return (v[:, :, ht * 128:(ht + 1) * 128], f"ffs#d#{cp}#{ht}")

            dequant_tile(down_pk[ht * 128:(ht + 1) * 128, cp * 1024:(cp + 1) * 1024],
                         down_sc[ht * 128:(ht + 1) * 128, cp * 64:(cp + 1) * 64], ht * 8 + cp, ddst)

    P.wait_keys("gpsimd", shard_keys)
    _snap0 = P.eng["gpsimd"].count
    P.op("gpsimd", lambda e: e.collective_compute(
        "AllGather", ALU.bypass, replica_groups=[list(range(8))],
        ins=[ffn_shard[:].opt()], outs=[ffn_full[:].opt()]),
        shard_keys, [ffn_full])

    if debug:
        P.dma("c_spill", out=dbg["d_wqT"][:], in_=wqT[:], reads=[wqT])
        P.dma("c_spill", out=dbg["d_ffn"][:], in_=ffn_full[:], reads=[ffn_full])
        P.dma("c_spill", out=dbg["d_woT"][:], in_=woT_scr[:], reads=[f"woT#{t}" for t in range(16)])

    P.barrier_all(limit={"gpsimd": _snap0})
    free_group("p0")

    if phases >= 1:
        xtile = [sb(f"xtile{i}", [128, EE], F32, "p1") for i in range(2)]
        lnscr = sb("lnscr", [128, EE], F32, "p1")
        ln_s = [sb(f"ln_s{i}", [128, 1], F32, "p1") for i in range(2)]
        ln_ss = [sb(f"ln_ss{i}", [128, 1], F32, "p1") for i in range(2)]
        ln_mu = [sb(f"ln_mu{i}", [128, 1], F32, "p1") for i in range(2)]
        ln_m2 = [sb(f"ln_m2{i}", [128, 1], F32, "p1") for i in range(2)]
        ln_r = [sb(f"ln_r{i}", [128, 1], F32, "p1") for i in range(2)]
        xnt_tok = [sb(f"xnt_tok{i}", [128, EE], BF16, "p1") for i in range(2)]
        xnT_cur = sb("xnT_cur", [128, ECH * BLK], BF16, "p1")
        ropec = [sb(f"ropecs{i}", [64, BLK], F32, "p1") for i in range(2)]
        ropes = [sb(f"ropess{i}", [64, BLK], F32, "p1") for i in range(2)]
        rp1 = [sb(f"rp1_{i}", [64, BLK], F32, "p1") for i in range(2)]
        rp2 = [sb(f"rp2_{i}", [64, BLK], F32, "p1") for i in range(2)]
        qspill = [sb(f"qspill{i}", [128, BLK], BF16, "p1") for i in range(4)]
        nsq = sb("nsq", [128, BLK], BF16, "p1")
        nmax = sb("nmax", [1, 1], F32, "p1")
        nmax_q = sb("nmax_q", [1, 1], F32, "p1")
        nmax_k = sb("nmax_k", [1, 1], F32, "p1")
        ntmp = sb("ntmp", [1, 1], F32, "p1")

        def ln_block(x_src_rows, ti):
            par = ti % 2
            xt = xtile[ti % 2]
            sc, s_, ss, mu, m2, r_ = lnscr, ln_s[par], ln_ss[par], ln_mu[par], ln_m2[par], ln_r[par]
            xo = xnt_tok[par]
            P.dma("c_x", out=xt[:], in_=x_src_rows, writes=[xt])
            P.op("vector", lambda e, xt=xt, s_=s_: e.tensor_reduce(s_[:], xt[:], AX.X, ALU.add), [xt], [s_])
            P.op("scalar", lambda e, xt=xt, sc=sc, ss=ss: e.activation(sc[:], xt[:], AF.Square, accum_out=ss[:]), [xt], [sc, ss])
            P.op("vector", lambda e, s_=s_, mu=mu: e.tensor_scalar(mu[:], s_[:], 1.0 / EE, None, ALU.mult), [s_], [mu])
            P.op("vector", lambda e, mu=mu, m2=m2: e.tensor_scalar(m2[:], mu[:], mu[:], 1e-5, ALU.mult, ALU.subtract), [mu], [m2])
            P.op("vector", lambda e, ss=ss, m2=m2: e.scalar_tensor_tensor(m2[:], ss[:], 1.0 / EE, m2[:], ALU.mult, ALU.subtract), [ss, m2], [m2])
            P.op("scalar", lambda e, m2=m2: e.activation(m2[:], m2[:], AF.Sqrt), [m2], [m2])
            P.op("vector", lambda e, m2=m2, r_=r_: e.reciprocal(r_[:], m2[:]), [m2], [r_])
            P.op("vector", lambda e, xt=xt, xo=xo, mu=mu, r_=r_: e.tensor_scalar(xo[:], xt[:], mu[:], r_[:], ALU.subtract, ALU.mult), [xt, mu, r_], [xo])
            return xo

        trring = [psA, psB, PS[3]]

        def transpose_to(xsrc, dst, col_base, tsub):
            for ec in range(ECH):
                ps = trring[ec % 3]
                P.op("tensor", lambda e, ec=ec, ps=ps, xsrc=xsrc: e.transpose(ps[:, 0:128], xsrc[:, ec * 128:(ec + 1) * 128], identb[:]),
                     [xsrc, identb], [ps])
                c0 = col_base(ec) + tsub * 128
                P.op("vector", lambda e, ps=ps, c0=c0, dst=dst: e.tensor_copy(dst[:, c0:c0 + 128], ps[:, 0:128]), [ps], [dst])

        def rope_apply(ps_src, pskey, dst, dst_key, c0, c1, rc, rs, par):
            r1, r2 = rp1[par], rp2[par]
            P.op("vector", lambda e, r1=r1: e.tensor_tensor(r1[:], ps_src[0:64, :], rc[0:64, :], ALU.mult), [pskey, rc], [r1])
            P.op("vector", lambda e, r2=r2: e.tensor_tensor(r2[:], ps_src[64:128, :], rs[0:64, :], ALU.mult), [pskey, rs], [r2])
            P.op("vector", lambda e, r1=r1, r2=r2: e.tensor_tensor(dst[0:64, c0:c1], r1[:], r2[:], ALU.subtract), [r1, r2], [dst_key])
            P.op("vector", lambda e, r1=r1: e.tensor_tensor(r1[:], ps_src[64:128, :], rc[0:64, :], ALU.mult), [pskey, rc], [r1])
            P.op("vector", lambda e, r2=r2: e.tensor_tensor(r2[:], ps_src[0:64, :], rs[0:64, :], ALU.mult), [pskey, rs], [r2])
            P.op("vector", lambda e, r1=r1, r2=r2: e.tensor_tensor(dst[64:128, c0:c1], r1[:], r2[:], ALU.add), [r1, r2], [dst_key])

        def norm_track(src_ap, src_keys, tgt):
            P.op("vector", lambda e: e.tensor_tensor(nsq[:], src_ap, src_ap, ALU.mult), src_keys, [nsq])
            P.op("tensor", lambda e: e.matmul(psS[0:1, 0:BLK], onescol_b[:], nsq[:], start=True, stop=True), [onescol_b, nsq], [psS_k])
            P.op("vector", lambda e: e.tensor_reduce(nmax[:], psS[0:1, 0:BLK], AX.X, ALU.max), [psS_k], [nmax])
            P.op("vector", lambda e, tgt=tgt: e.tensor_tensor(tgt[:], tgt[:], nmax[:], ALU.max), [nmax, tgt], [tgt])

        P.op("vector", lambda e: e.memset(nmax_q[:], 0.0), [], [nmax_q])
        P.op("vector", lambda e: e.memset(nmax_k[:], 0.0), [], [nmax_k])
        psV, psV_k = PSF[2], PS[2]
        psQ2, psQ2_k = PSF[4], PS[4]
        for tb in range(NB):
            rc = ropec[tb % 2]
            rs = ropes[tb % 2]
            P.dma("c_rope", out=rc[:], in_=ropec_t[:, tb * BLK:(tb + 1) * BLK], writes=[rc])
            P.dma("c_rope", out=rs[:], in_=ropes_t[:, tb * BLK:(tb + 1) * BLK], writes=[rs])
            for tsub in range(4):
                r0 = tb * BLK + tsub * 128
                xo = ln_block(x_full[r0:r0 + 128, :], tb * 4 + tsub)
                transpose_to(xo, xnT_cur, lambda ec: ec * BLK, tsub)
                fns = []
                for ec in range(ECH):
                    fns.append(lambda e, ec=ec, tsub=tsub: e.matmul(
                        psV[:, 0:128], xnT_cur[:, ec * BLK + tsub * 128: ec * BLK + tsub * 128 + 128],
                        wvT[:, ec * 128:(ec + 1) * 128], start=(ec == 0), stop=False))
                fns.append(lambda e: e.matmul(
                    psV[:, 0:128], onesrow_b[0:1, 0:128], bv_sb[:], start=False, stop=True))
                P.op_group("tensor", fns, [xnT_cur, wvT, onesrow_b, bv_sb], [psV_k])
                kb_i = tb * 4 + tsub
                P.op("vector", lambda e, kb_i=kb_i: e.tensor_copy(V_sb[:, kb_i * 128:(kb_i + 1) * 128], psV[:, 0:128]), [psV_k], [V_sb])
            fns = []
            for ec in range(ECH):
                fns.append(lambda e, ec=ec: e.matmul(
                    psM[:, 0:BLK], wkT[:, ec * 128:(ec + 1) * 128], xnT_cur[:, ec * BLK:(ec + 1) * BLK],
                    start=(ec == 0), stop=False))
            fns.append(lambda e: e.matmul(psM[:, 0:BLK], bk_sb[:], onesrow_b[:], start=False, stop=True))
            P.op_group("tensor", fns, [xnT_cur, wkT, bk_sb, onesrow_b], [psM_k])
            rope_apply(psM[:, 0:BLK], psM_k, K_sb, K_sb, tb * BLK, (tb + 1) * BLK, rc, rs, 0)
            for hc in range(4):
                qps, qps_k = (psU, psU_k) if hc % 2 == 0 else (psQ2, psQ2_k)
                qs = qspill[hc]
                fns = []
                for ec in range(ECH):
                    fns.append(lambda e, ec=ec, hc=hc, qps=qps: e.matmul(
                        qps[:, 0:BLK], wqT[:, ec * 512 + hc * 128: ec * 512 + (hc + 1) * 128],
                        xnT_cur[:, ec * BLK:(ec + 1) * BLK], start=(ec == 0), stop=False))
                fns.append(lambda e, hc=hc, qps=qps: e.matmul(qps[:, 0:BLK], bq_sb[0:1, hc * 128:(hc + 1) * 128], onesrow_b[:], start=False, stop=True))
                P.op_group("tensor", fns, [xnT_cur, wqT, bq_sb, onesrow_b], [qps_k])
                rope_apply(qps[:, 0:BLK], qps_k, qs, qs, 0, BLK, rc, rs, hc % 2)
                P.dma("c_spill", out=qT_scr[hc * 128:(hc + 1) * 128, tb * BLK:(tb + 1) * BLK], in_=qs[:],
                      reads=[qs], writes=[f"qT#{hc}#{tb}"])
            norm_track(K_sb[:, tb * BLK:(tb + 1) * BLK], [K_sb], nmax_k)
            for hc in range(4):
                norm_track(qspill[hc][:], [qspill[hc]], nmax_q)
        for tb in range(4):
            for tsub in range(4):
                r0 = tb * BLK + tsub * 128
                xo = ln_block(x_own[r0:r0 + 128, :], tb * 4 + tsub)
                transpose_to(xo, xnT_own, lambda ec, tb=tb: ec * 2048 + tb * BLK, tsub)
        P.op("vector", lambda e: e.tensor_tensor(ntmp[:], nmax_q[:], nmax_k[:], ALU.mult), [nmax_q, nmax_k], [ntmp])
        P.op("scalar", lambda e: e.activation(ntmp[:], ntmp[:], AF.Sqrt, scale=SCALE * SCALE), [ntmp], [ntmp])
        P.op("vector", lambda e: e.tensor_scalar(ntmp[:], ntmp[:], -1.0, None, ALU.mult), [ntmp], [ntmp])
        P.op("tensor", lambda e: e.matmul(psR[:, 0:1], onesrow_f[0:1, :], ntmp[:], start=True, stop=True), [onesrow_f, ntmp], [psR_k])
        P.op("vector", lambda e: e.tensor_copy(negB[:], psR[:, 0:1]), [psR_k], [negB])

        if debug:
            P.dma("c_spill", out=dbg["d_kT"][:], in_=K_sb[:], reads=[K_sb])
            P.dma("c_spill", out=dbg["d_v"][:], in_=V_sb[:], reads=[V_sb])
            P.dma("c_spill", out=dbg["d_xnT"][:], in_=xnT_own[:], reads=[xnT_own])
            P.dma("c_spill", out=dbg["d_B"][:], in_=negB[:], reads=[negB])
            qv = dbg["d_qT"]
            for hc in range(4):
                for tb2 in range(NB):
                    P.dma("c_spill", out=qv[hc * 128:(hc + 1) * 128, tb2 * BLK:(tb2 + 1) * BLK],
                          in_=qT_scr[hc * 128:(hc + 1) * 128, tb2 * BLK:(tb2 + 1) * BLK],
                          reads=[f"qT#{hc}#{tb2}"], writes=[f"dq#{hc}#{tb2}"])

        P.barrier_all()
        free_group("p1")
        free_group("p1x")

    if phases >= 2:
        qh_buf = [sb(f"qh{i}", [128, BLK], BF16, "p2") for i in range(2)]
        pt_buf = [sb(f"pt{i}", [128, BLK], BF16, "p2") for i in range(4)]
        recip_sb = sb("recip_sb", [1, BLK], F32, "p2")
        rb_sb = sb("rb_sb", [128, BLK], F32, "p2")
        oh_sb = sb("oh_sb", [128, BLK], BF16, "p2")
        a2a_keys = []
        pending = []
        for hc in range(4):
            for qt in range(NB):
                qb = qh_buf[qt % 2]
                if qt == 0:
                    P.dma("c_q", out=qb[:], in_=qT_scr[hc * 128:(hc + 1) * 128, qt * BLK:(qt + 1) * BLK],
                          reads=[f"qT#{hc}#{qt}"], writes=[qb])
                nkc = (qt + 1) * 4
                if qt + 1 < NB:
                    nqb = qh_buf[(qt + 1) % 2]
                    P.dma("c_q", out=nqb[:], in_=qT_scr[hc * 128:(hc + 1) * 128, (qt + 1) * BLK:(qt + 2) * BLK],
                          reads=[f"qT#{hc}#{qt + 1}"], writes=[nqb])
                elif hc + 1 < 4:
                    nqb = qh_buf[0]
                    P.dma("c_q", out=nqb[:], in_=qT_scr[(hc + 1) * 128:(hc + 2) * 128, 0:BLK],
                          reads=[f"qT#{hc + 1}#0"], writes=[nqb])
                psO, psO_k = (PSF[4], PS[4]) if qt % 2 == 0 else (PSF[6], PS[6])
                psS_f, psS_k = (PSF[5], PS[5]) if qt % 2 == 0 else (PSF[7], PS[7])
                psS = psS_f[0:1, :]

                def emit_score(kb, qb=qb, qt=qt):
                    ps, psk = PSF[kb % 4], PS[kb % 4]
                    pt = pt_buf[kb % 4]
                    P.op("tensor", lambda e, kb=kb, ps=ps, qb=qb: e.matmul(
                        ps[:, 0:BLK], K_sb[:, kb * 128:(kb + 1) * 128], qb[:], start=True, stop=True),
                        [K_sb, qb], [psk])
                    P.op("scalar", lambda e, ps=ps, pt=pt: e.activation(pt[:], ps[:, 0:BLK], AF.Exp, bias=negB[:], scale=SCALE),
                         [psk, negB], [pt])
                    d = kb * 128 - qt * BLK
                    if d >= 0:
                        off = 384 - d
                        P.op("vector", lambda e, pt=pt, off=off: e.tensor_tensor(pt[:], pt[:], maskm[:, off:off + BLK], ALU.mult),
                             [pt, maskm], [pt])

                def emit_pv(kb, nkc=nkc, psO=psO, psS=psS, psO_k=psO_k, psS_k=psS_k):
                    pt = pt_buf[kb % 4]
                    fns = [
                        lambda e, kb=kb, pt=pt, nkc=nkc, psO=psO: e.matmul(
                            psO[:, 0:BLK], V_sb[:, kb * 128:(kb + 1) * 128], pt[:], start=(kb == 0), stop=(kb == nkc - 1)),
                        lambda e, kb=kb, pt=pt, nkc=nkc, psS=psS: e.matmul(
                            psS[0:1, 0:BLK], onescol_b[:], pt[:], start=(kb == 0), stop=(kb == nkc - 1)),
                    ]
                    P.op_group("tensor", fns, [V_sb, onescol_b, pt], [psO_k, psS_k])

                for kb0 in range(min(3, nkc)):
                    emit_score(kb0)
                for kb in range(nkc):
                    if kb + 3 < nkc:
                        emit_score(kb + 3)
                    emit_pv(kb)

                def division(hc=hc, qt=qt, psO=psO, psS=psS, psO_k=psO_k, psS_k=psS_k):
                    P.op("vector", lambda e, psS=psS: e.reciprocal(recip_sb[:], psS[0:1, 0:BLK]), [psS_k], [recip_sb])
                    P.op("tensor", lambda e: e.matmul(psR[:, 0:BLK], onesrow_f[:], recip_sb[:], start=True, stop=True),
                         [onesrow_f, recip_sb], [psR_k])
                    P.op("vector", lambda e: e.tensor_copy(rb_sb[:], psR[:, 0:BLK]), [psR_k], [rb_sb])
                    P.op("vector", lambda e, psO=psO: e.tensor_tensor(oh_sb[:], psO[:, 0:BLK], rb_sb[:], ALU.mult), [psO_k, rb_sb], [oh_sb])
                    qu, qc = qt // 4, qt % 4
                    for piece in (qu, qu + 4):
                        key = f"a2a#{piece}#{hc}#{qt}"
                        a2a_keys.append(key)
                        P.dma("c_spill", out=a2a_in[piece, hc * 128:(hc + 1) * 128, qc * BLK:(qc + 1) * BLK],
                              in_=oh_sb[:], reads=[oh_sb], writes=[key])

                pending.append(division)
                if len(pending) > 1:
                    pending.pop(0)()
        for fn in pending:
            fn()
        P.wait_keys("gpsimd", a2a_keys)
        _snap2 = P.eng["gpsimd"].count
        from concourse.aot_env import allow_ici_all_to_all

        def do_a2a(e):
            with allow_ici_all_to_all():
                return e.collective_compute(
                    "AllToAll", ALU.bypass, replica_groups=[list(range(8))],
                    ins=[a2a_in[:].opt()], outs=[a2a_out[:].opt()])

        P.op("gpsimd", do_a2a, a2a_keys, [a2a_out])
        if debug:
            P.dma("c_spill", out=dbg["d_a2a"][:], in_=a2a_out[:], reads=[a2a_out])
        P.barrier_all(limit={"gpsimd": _snap2})
        free_group("p2")
        free_group("p12")

    if phases >= 3:
        oTa = [sb(f"oTa{i}", [128, BLK], BF16, "p3") for i in range(2)]
        oTb = [sb(f"oTb{i}", [128, BLK], BF16, "p3") for i in range(2)]
        oT_qt = sb("oT_qt", [128, 16 * BLK], BF16, "p3")
        woT_buf = [sb(f"woTb{i}", [128, 16 * 128], BF16, "p3") for i in range(2)]
        wgu_buf = [sb(f"wgu{i}", [128, 32 * 128], BF16, "p3") for i in range(2)]
        wd_buf = [sb(f"wdb{i}", [128, 16 * 256], BF16, "p3") for i in range(2)]
        inter = [sb(f"inter{i}", [128, BLK], BF16, "p3") for i in range(16)]
        gelu_s = [sb(f"gelu_s{i}", [128, BLK], F32, "p3") for i in range(2)]
        oTt = gelu_s[0]
        acc = sb("acc", [128, 16 * BLK], F32, "p3")
        xo_tile = sb("xo0", [128, EE], F32, "p3")
        out_sb = sb("outsb0", [128, EE], F32, "p3")

        woT_v2 = woT_scr[:].rearrange("(oc p c) -> oc p c", oc=16, p=128)
        for qt in range(4):
            def do_merge(qt=qt):
                for hd in range(16):
                    rr, sub = hd // 4, hd % 4
                    ta = oTa[hd % 2]
                    tb_ = oTb[hd % 2]
                    P.dma("c_oT", out=ta[:], in_=a2a_out[rr, sub * 128:(sub + 1) * 128, qt * BLK:(qt + 1) * BLK],
                          reads=[a2a_out], writes=[ta])
                    P.dma("c_oT", out=tb_[:], in_=a2a_out[rr + 4, sub * 128:(sub + 1) * 128, qt * BLK:(qt + 1) * BLK],
                          reads=[a2a_out], writes=[tb_])
                    P.op("vector", lambda e, ta=ta: e.tensor_scalar(oTt[:], ta[:], sel0[:], None, ALU.mult), [ta, sel0], [oTt])
                    P.op("vector", lambda e, tb_=tb_, hd=hd: e.scalar_tensor_tensor(
                        oT_qt[:, hd * BLK:(hd + 1) * BLK], tb_[:], sel1[:], oTt[:], ALU.mult, ALU.add),
                        [tb_, sel1, oTt], [oT_qt])

            if qt > 0:
                do_merge()
            for grp in range(8):
                for nch in range(16):
                    wgb = wgu_buf[nch % 2]
                    gps, gpk = (psM, psM_k) if nch % 2 == 0 else (psR, psR_k)
                    ups, upk = (psU, psU_k) if nch % 2 == 0 else (psO, psO_k)
                    src_gu = ffn_full[grp * SH + nch * 524288: grp * SH + (nch + 1) * 524288]
                    P.dma("c_fw", out=wgb[:].rearrange("p (gu c) -> p gu c", gu=2),
                          in_=src_gu.rearrange("(gu p c) -> p gu c", gu=2, p=128), reads=[ffn_full], writes=[wgb])
                    fns = []
                    for ec in range(ECH):
                        fns.append(lambda e, ec=ec, wgb=wgb, qt=qt, gps=gps: e.matmul(
                            gps[:, 0:BLK], wgb[:, ec * 128:(ec + 1) * 128], xnT_own[:, ec * 2048 + qt * BLK: ec * 2048 + (qt + 1) * BLK],
                            start=(ec == 0), stop=(ec == ECH - 1)))
                    P.op_group("tensor", fns, [wgb, xnT_own], [gpk])
                    fns = []
                    for ec in range(ECH):
                        fns.append(lambda e, ec=ec, wgb=wgb, qt=qt, ups=ups: e.matmul(
                            ups[:, 0:BLK], wgb[:, 2048 + ec * 128:(ec + 1) * 128 + 2048], xnT_own[:, ec * 2048 + qt * BLK: ec * 2048 + (qt + 1) * BLK],
                            start=(ec == 0), stop=(ec == ECH - 1)))
                    P.op_group("tensor", fns, [wgb, xnT_own], [upk])
                    it = inter[nch]
                    gl = gelu_s[nch % 2]
                    P.op("scalar", lambda e, gps=gps, gl=gl: e.activation(gl[:], gps[:, 0:BLK], AF.Gelu_apprx_tanh), [gpk], [gl])
                    P.op("vector", lambda e, it=it, ups=ups, gl=gl: e.tensor_tensor(it[:], gl[:], ups[:, 0:BLK], ALU.mult), [gl, upk], [it])
                if qt == 0 and grp == 3:
                    do_merge()
                for sh in range(8):
                    wdb = wd_buf[sh % 2]
                    src = ffn_full[sh * SH + 2 * SH_G + grp * 524288: sh * SH + 2 * SH_G + (grp + 1) * 524288]
                    P.dma("c_fd", out=wdb[:], in_=src.rearrange("(p c) -> p c", p=128), reads=[ffn_full], writes=[wdb])
                    for hcl in range(2):
                        hc_g = sh * 2 + hcl
                        ps, pskey = (psA_f, psA) if hcl % 2 == 0 else (psB_f, psB)
                        fns = []
                        for ncl in range(16):
                            fns.append(lambda e, ncl=ncl, hcl=hcl, ps=ps, wdb=wdb: e.matmul(
                                ps[:, 0:BLK], wdb[:, ncl * 256 + hcl * 128: ncl * 256 + (hcl + 1) * 128],
                                inter[ncl][:], start=(ncl == 0), stop=(ncl == 15)))
                        P.op_group("tensor", fns, [wdb] + [inter[ncl] for ncl in range(16)], [pskey])
                        if grp == 0:
                            P.op("vector", lambda e, hc_g=hc_g, ps=ps: e.tensor_copy(
                                acc[:, hc_g * BLK:(hc_g + 1) * BLK], ps[:, 0:BLK]),
                                [pskey], [f"acc#{hc_g}"])
                        else:
                            P.op("vector", lambda e, hc_g=hc_g, ps=ps: e.tensor_tensor(
                                acc[:, hc_g * BLK:(hc_g + 1) * BLK], acc[:, hc_g * BLK:(hc_g + 1) * BLK], ps[:, 0:BLK], ALU.add),
                                [pskey, f"acc#{hc_g}"], [f"acc#{hc_g}"])
            for oc in range(16):
                wob = woT_buf[oc % 2]
                P.dma("c_oT", out=wob[:], in_=woT_v2[oc], reads=[f"woT#{oc}"], writes=[wob])
                ps, pskey = (psA_f, psA) if oc % 2 == 0 else (psB_f, psB)
                fns = []
                for hd in range(16):
                    fns.append(lambda e, hd=hd, ps=ps, wob=wob: e.matmul(
                        ps[:, 0:BLK], wob[:, hd * 128:(hd + 1) * 128], oT_qt[:, hd * BLK:(hd + 1) * BLK],
                        start=(hd == 0), stop=False))
                fns.append(lambda e, oc=oc, ps=ps: e.matmul(
                    ps[:, 0:BLK], bo_sb[0:1, oc * 128:(oc + 1) * 128], onesrow_b[:], start=False, stop=True))
                P.op_group("tensor", fns, [wob, oT_qt, bo_sb, onesrow_b], [pskey])
                P.op("vector", lambda e, oc=oc, ps=ps: e.tensor_tensor(
                    acc[:, oc * BLK:(oc + 1) * BLK], acc[:, oc * BLK:(oc + 1) * BLK], ps[:, 0:BLK], ALU.add),
                    [pskey, f"acc#{oc}"], [f"acc#{oc}"])
            for tsub in range(4):
                r0 = qt * BLK + tsub * 128
                P.dma("c_xo", out=xo_tile[:], in_=x_own[r0:r0 + 128, :], writes=[xo_tile])
                for oc in range(16):
                    trp, trk = (psM, psM_k) if oc % 2 == 0 else (psU, psU_k)
                    P.op("tensor", lambda e, oc=oc, tsub=tsub, trp=trp: e.transpose(
                        trp[:, 0:128], acc[:, oc * BLK + tsub * 128: oc * BLK + (tsub + 1) * 128], identf[:]),
                        [f"acc#{oc}", identf], [trk], pe_relaxed=False)
                    P.op("vector", lambda e, oc=oc, trp=trp: e.tensor_tensor(
                        out_sb[:, oc * 128:(oc + 1) * 128], trp[:, 0:128], xo_tile[:, oc * 128:(oc + 1) * 128], ALU.add),
                        [trk, xo_tile], [out_sb])
                P.dma("c_out", out=out_t[r0:r0 + 128, :], in_=out_sb[:], reads=[out_sb], writes=[f"out#{qt}#{tsub}"])

        P.wait_tokens("gpsimd", [P.barrier_token("c_out")])
    if debug:
        P.wait_tokens("gpsimd", [P.barrier_token("c_spill")])

    P.finalize()
    for h in reversed(alloc_order):
        h.__exit__(None, None, None)
    for h in reversed(groups.pop("psum", [])):
        h.__exit__(None, None, None)
    return nc



def _host_constants():
    identb = np.eye(128, dtype=np.float32).astype(ml_dtypes.bfloat16)
    identf = np.eye(128, dtype=np.float32)
    inv_freq = (1.0 / 10000.0 ** (np.arange(0, DH, 2, dtype=np.float32) / DH)).astype(np.float32)
    t = np.arange(SS, dtype=np.float32)
    ang = np.outer(inv_freq, t)
    ropec = np.cos(ang).astype(np.float32)
    ropes = np.sin(ang).astype(np.float32)
    i = np.arange(128)[:, None]
    u = np.arange(896)[None, :]
    maskm = ((u - i) >= 384).astype(np.float32).astype(ml_dtypes.bfloat16)
    return identb, identf, ropec, ropes, maskm


_CACHE = {}


def kernel(L0_attn_x, L0_attn_wq, L0_attn_bq, L0_attn_wk, L0_attn_bk,
           L0_attn_wv, L0_attn_bv, L0_attn_wo, L0_attn_bo,
           L0_ffn_gate_W_packed, L0_ffn_gate_scales,
           L0_ffn_up_W_packed, L0_ffn_up_scales,
           L0_ffn_down_W_packed, L0_ffn_down_scales,
           _debug=0, _phases=3):
    x = np.ascontiguousarray(np.asarray(L0_attn_x, dtype=np.float32))
    wq = np.asarray(L0_attn_wq, dtype=np.float32)
    bq = np.asarray(L0_attn_bq, dtype=np.float32)
    wk = np.asarray(L0_attn_wk, dtype=np.float32)
    bk = np.asarray(L0_attn_bk, dtype=np.float32)
    wv = np.asarray(L0_attn_wv, dtype=np.float32)
    bv = np.asarray(L0_attn_bv, dtype=np.float32)
    wo = np.asarray(L0_attn_wo, dtype=np.float32)
    bo = np.asarray(L0_attn_bo, dtype=np.float32)
    gpk = np.asarray(L0_ffn_gate_W_packed, dtype=np.int32)
    gsc = np.asarray(L0_ffn_gate_scales, dtype=np.float32)
    upk = np.asarray(L0_ffn_up_W_packed, dtype=np.int32)
    usc = np.asarray(L0_ffn_up_scales, dtype=np.float32)
    dpk = np.asarray(L0_ffn_down_W_packed, dtype=np.int32)
    dsc = np.asarray(L0_ffn_down_scales, dtype=np.float32)

    identb, identf, ropec, ropes, maskm = _host_constants()

    key = (_debug, _phases)
    if key not in _CACHE:
        _CACHE[key] = build(debug=_debug, phases=_phases)
    nc = _CACHE[key]

    in_maps = []
    for c in range(8):
        b, g = c // 4, c % 4
        sel0 = np.full((128, 1), 1.0 if b == 0 else 0.0, np.float32)
        in_maps.append(dict(
            x_full=x[b], x_own=np.ascontiguousarray(x[b, g * 2048:(g + 1) * 2048]),
            wq_g=np.ascontiguousarray(wq[g * 512:(g + 1) * 512]),
            bq_g=np.ascontiguousarray(bq[g * 512:(g + 1) * 512]).reshape(1, 512),
            wk_g=np.ascontiguousarray(wk[g * 128:(g + 1) * 128]),
            bk_g=np.ascontiguousarray(bk[g * 128:(g + 1) * 128]).reshape(1, 128),
            wv_g=np.ascontiguousarray(wv[g * 128:(g + 1) * 128]),
            bv_g=np.ascontiguousarray(bv[g * 128:(g + 1) * 128]).reshape(1, 128),
            wo=wo, bo=bo.reshape(1, EE),
            gate_pk=np.ascontiguousarray(gpk[c * GSH:(c + 1) * GSH]),
            gate_sc=np.ascontiguousarray(gsc[c * GSH:(c + 1) * GSH]),
            up_pk=np.ascontiguousarray(upk[c * GSH:(c + 1) * GSH]),
            up_sc=np.ascontiguousarray(usc[c * GSH:(c + 1) * GSH]),
            down_pk=np.ascontiguousarray(dpk[c * DSH:(c + 1) * DSH]),
            down_sc=np.ascontiguousarray(dsc[c * DSH:(c + 1) * DSH]),
            identb=identb, identf=identf, ropec=ropec, ropes=ropes,
            maskm=maskm, sel0=sel0, sel1=(1.0 - sel0).astype(np.float32),
        ))

    res = run_bass_kernel_spmd(nc, in_maps, core_ids=list(range(8)))
    if _debug:
        return res
    out = np.empty((BB, SS, EE), np.float32)
    for c in range(8):
        b, g = c // 4, c % 4
        out[b, g * 2048:(g + 1) * 2048] = res.results[c]["out"]
    return out
